# version 22
# baseline (speedup 1.0000x reference)
import math
import numpy as np
import ml_dtypes
import concourse.bass as bass
import concourse.mybir as mybir
from concourse.bass_utils import run_bass_kernel_spmd

F32 = mybir.dt.float32
BF16 = mybir.dt.bfloat16
AF = mybir.ActivationFunctionType
ALU = mybir.AluOpType
AX = mybir.AxisListType

A_TILES = 20
NT = 32
D = 1024
SEQ = 4096
INV_DK = 1.0 / 32.0


class Emitter:

    DMA_DEPTH = 6

    def __init__(self, nc, sems):
        self.nc = nc
        self.sems = sems
        self.cnt = {k: 0 for k in sems}
        self.lastw = {}
        self.readers = {}
        self.waited = {}
        self.dmasub = {"dx": 0, "dw": 0}
        self.eng = {
            "t": nc.tensor, "v": nc.vector, "s": nc.scalar,
            "y": nc.sync, "g": nc.gpsimd,
        }

    def _semname(self, e, dma):
        if e == "y":
            return "dx"
        if e == "g":
            return "dw" if dma else "gc"
        return {"t": "tt", "v": "vv", "s": "ss"}[e]

    def _wait(self, e, tok):
        sn, val = tok
        w = self.waited.setdefault(e, {})
        if w.get(sn, 0) >= val:
            return
        self.eng[e].wait_ge(self.sems[sn], val)
        w[sn] = val

    def _deps(self, e, reads, writes):
        toks = []
        for r in reads:
            t = self.lastw.get(r)
            if t:
                toks.append(t)
        for wkey in writes:
            t = self.lastw.get(wkey)
            if t:
                toks.append(t)
            toks.extend(self.readers.get(wkey, ()))
        best = {}
        for sn, val in toks:
            if best.get(sn, 0) < val:
                best[sn] = val
        for sn, val in best.items():
            self._wait(e, (sn, val))

    def _finish(self, e, inst, reads, writes, dma):
        sn = self._semname(e, dma)
        inc = 16 if dma else 1
        inst.then_inc(self.sems[sn], inc)
        self.cnt[sn] += inc
        tok = (sn, self.cnt[sn])
        for wkey in writes:
            self.lastw[wkey] = tok
            self.readers[wkey] = []
        for r in reads:
            self.readers.setdefault(r, []).append(tok)
        return tok

    def op(self, e, fn, reads=(), writes=(), dma=False):
        self._deps(e, reads, writes)
        if dma:
            q = self._semname(e, True)
            sn = f"{q}{self.dmasub[q]}"
            self.dmasub[q] = (self.dmasub[q] + 1) % self.DMA_DEPTH
            if self.cnt[sn] > 0:
                self._wait(e, (sn, self.cnt[sn]))
            inst = fn(self.eng[e])
            inst.then_inc(self.sems[sn], 16)
            self.cnt[sn] += 16
            tok = (sn, self.cnt[sn])
            for wkey in writes:
                self.lastw[wkey] = tok
                self.readers[wkey] = []
            for r in reads:
                self.readers.setdefault(r, []).append(tok)
            return tok
        inst = fn(self.eng[e])
        return self._finish(e, inst, reads, writes, dma)

    def group(self, e, fns, reads=(), writes=()):
        self._deps(e, reads, writes)
        inst = None
        for fn in fns:
            inst = fn(self.eng[e])
        return self._finish(e, inst, reads, writes, False)

    def seal(self, keys, src_keys=None):
        if src_keys is None:
            toks = [(sn, v) for sn, v in self.cnt.items() if v > 0]
        else:
            best = {}
            for k2 in src_keys:
                cand = list(self.readers.get(k2, ()))
                t = self.lastw.get(k2)
                if t:
                    cand.append(t)
                for sn, v in cand:
                    if best.get(sn, 0) < v:
                        best[sn] = v
            toks = list(best.items())
        for k in keys:
            self.readers[k] = list(toks)
            self.lastw.pop(k, None)


_CACHE = {}


def build_program(single_body=None):
    nc = bass.Bass("TRN2", target_bir_lowering=False)

    xTg = nc.dram_tensor("xTg", [128, 8 * SEQ], F32, kind="ExternalInput")
    wq = nc.dram_tensor("wq", [D, D], F32, kind="ExternalInput")
    wk = nc.dram_tensor("wk", [D, D], F32, kind="ExternalInput")
    wv = nc.dram_tensor("wv", [D, D], F32, kind="ExternalInput")
    wt = nc.dram_tensor("wt", [D, D], F32, kind="ExternalInput")
    vrd = nc.dram_tensor("vr", [D, 32], F32, kind="ExternalInput")
    ctd = nc.dram_tensor("ct", [32, D], F32, kind="ExternalInput")
    biasd = nc.dram_tensor("biasp", [128, 32], F32, kind="ExternalInput")
    maskd = nc.dram_tensor("maskp", [128, 512], F32, kind="ExternalInput")
    identd = nc.dram_tensor("ident", [128, 128], F32, kind="ExternalInput")
    outT = nc.dram_tensor("outT", [D, SEQ], F32, kind="ExternalOutput")

    wq_s = nc.alloc_sbuf_tensor("wq_s", [128, 8192], BF16)
    wk_s = nc.alloc_sbuf_tensor("wk_s", [128, 8192], BF16)
    wv_s = nc.alloc_sbuf_tensor("wv_s", [128, 8192], BF16)
    wt_s = nc.alloc_sbuf_tensor("wt_s", [128, 8192], BF16)
    vr_s = nc.alloc_sbuf_tensor("vr_s", [128, 256], BF16)
    ct_s = nc.alloc_sbuf_tensor("ct_s", [32, 1024], BF16)
    bias_s = nc.alloc_sbuf_tensor("bias_s", [128, 32], F32)
    mask_s = nc.alloc_sbuf_tensor("mask_s", [128, 512], BF16)
    ident_s = nc.alloc_sbuf_tensor("ident_s", [128, 128], F32)
    onesd = nc.alloc_sbuf_tensor("onesd", [128, 128], BF16)
    ones1f = nc.alloc_sbuf_tensor("ones1f", [1, 128], F32)
    epsv = nc.alloc_sbuf_tensor("epsv", [128, 1], F32)
    rbig = nc.alloc_sbuf_tensor("rbig", [128, 45056], BF16)
    ab = nc.alloc_sbuf_tensor("ab", [128, 16384], BF16)
    af = nc.alloc_sbuf_tensor("af", [128, 4608], F32)
    cmax = nc.alloc_sbuf_tensor("cmax", [128, 1], F32)
    nmb = nc.alloc_sbuf_tensor("nmb", [128, 2], F32)
    sumb = nc.alloc_sbuf_tensor("sumb", [128, 2], F32)
    csum = nc.alloc_sbuf_tensor("csum", [128, 2], F32)
    dbuf = nc.alloc_sbuf_tensor("dbuf", [128, 1], F32)
    fbuf = nc.alloc_sbuf_tensor("fbuf", [128, 1], F32)
    ezb = nc.alloc_sbuf_tensor("ezb", [128, 1], F32)
    numb = nc.alloc_sbuf_tensor("numb", [128, 1], F32)
    zib = nc.alloc_sbuf_tensor("zib", [128, 1], F32)
    rgrp = nc.alloc_sbuf_tensor("rgrp", [128, 4], F32)

    pb01 = nc.alloc_psum_tensor("pb01", [128, 1024], F32)
    pb23 = nc.alloc_psum_tensor("pb23", [128, 1024], F32)
    pbv = {0: pb01[:, 0:512], 1: pb01[:, 512:1024],
           2: pb23[:, 0:512], 3: pb23[:, 512:1024]}
    pb = {i: nc.alloc_psum_tensor(f"pb{i}", [128, 512], F32)
          for i in range(4, 8)}
    pbBIG = [pb01, pb23]

    sems = {
        "tt": nc.alloc_semaphore("tt"), "vv": nc.alloc_semaphore("vv"),
        "ss": nc.alloc_semaphore("ss"), "gc": nc.alloc_semaphore("gc"),
    }
    for q in ("dx", "dw"):
        for i in range(Emitter.DMA_DEPTH):
            sems[f"{q}{i}"] = nc.alloc_semaphore(f"{q}{i}")

    def emit_body(lo, hi):
        E = Emitter(nc, sems)
        kgn = (hi + 3) // 4
        og0 = lo // 4
        kcols = kgn * 512
        ownlen = 128 * (hi - lo)

        def kT(jc):
            return rbig[:, jc * kcols:(jc + 1) * kcols]

        def xno(dc):
            base = 8 * kcols
            return rbig[:, base + dc * ownlen: base + (dc + 1) * ownlen]

        def xbg(slot, dc):
            return ab[:, slot * 4096 + dc * 512: slot * 4096 + (dc + 1) * 512]

        def xbg_all(slot):
            return ab[:, slot * 4096:(slot + 1) * 4096]

        def xsq(dc):
            return ab[:, 8192 + dc * 512: 8192 + (dc + 1) * 512]
        xsq_all = ab[:, 8192:12288]

        def xnt(dc):
            return ab[:, 12288 + dc * 512: 12288 + (dc + 1) * 512]

        def qgrp(slot, jc):
            return ab[:, slot * 4096 + jc * 512: slot * 4096 + (jc + 1) * 512]
        t1b = ab[0:32, 8192:8704]

        def grotb(p):
            return ab[:, 8704 + p * 512: 8704 + (p + 1) * 512]
        Rsb = ab[:, 10240:10752]
        expo = ab[:, 10752:11776]

        def sAv(sl):
            return af[:, sl * 1536: sl * 1536 + 512]

        def sBv(sl):
            return af[:, sl * 1536 + 512: sl * 1536 + 1024]

        def sCv(sl):
            return af[:, sl * 1536 + 1024: sl * 1536 + 1536]
        sD0 = af[:, 3072:3584]
        sD1 = af[:, 3584:4096]

        def wstage(p):
            return af[:, 2048 + p * 512: 2048 + (p + 1) * 512]

        def vsb(p):
            return af[:, p * 512:(p + 1) * 512]

        def xres(p):
            return af[:, 1024 + p * 512: 1024 + (p + 1) * 512]

        def orot(p):
            return af[:, 2048 + p * 512: 2048 + (p + 1) * 512]

        def c1b(p):
            return af[:, 3072 + p * 512: 3072 + (p + 1) * 512]
        rrow = af[0:1, 4096:4608]

        E.op("g", lambda g: g.dma_start(out=epsv[0:1, 0:1], in_=xTg[0:1, 0:1]),
             writes=["epsv"], dma=True)
        E.op("v", lambda v: v.memset(onesd[:], 1.0 / 1024.0), writes=["onesd"])
        E.op("v", lambda v: v.memset(ones1f[:], 1.0), writes=["ones1f"])
        E.op("v", lambda v: v.memset(epsv[:], 1e-5), writes=["epsv"])
        E.op("y", lambda y: y.dma_start(out=bias_s[:], in_=biasd[:]),
             writes=["biasp"], dma=True)
        E.op("y", lambda y: y.dma_start(out=ident_s[:], in_=identd[:]),
             writes=["ident"], dma=True)
        for dc in range(8):
            for h in range(2):
                p = (2 * dc + h) % 4
                E.op("y", lambda y, dc=dc, h=h, p=p: y.dma_start(
                    out=wstage(p),
                    in_=wk[dc * 128:(dc + 1) * 128, h * 512:(h + 1) * 512]),
                    writes=[f"wst{p}"], dma=True)
                E.op("v", lambda v, dc=dc, h=h, p=p: v.tensor_copy(
                    out=wk_s[:, dc * 1024 + h * 512: dc * 1024 + (h + 1) * 512],
                    in_=wstage(p)),
                    reads=[f"wst{p}"], writes=["wk"])

        E.op("g", lambda g: g.dma_start(out=mask_s[:], in_=maskd[:]),
             writes=["maskp"], dma=True)

        wload = []
        for nm, dram, sb in (("wq", wq, wq_s),
                             ("wv", wv, wv_s), ("wt", wt, wt_s)):
            for dc in range(8):
                wload.append((nm, dram, sb, dc))

        def emit_wload(n):
            while wload and n > 0:
                nm, dram, sb, dc = wload.pop(0)
                E.op("g", lambda g, dram=dram, sb=sb, dc=dc: g.dma_start(
                    out=sb[:, dc * 1024:(dc + 1) * 1024],
                    in_=dram[dc * 128:(dc + 1) * 128, :]),
                    writes=[nm], dma=True)
                n -= 1

        def stats_part(gk):
            sl = gk % 2
            xbk = f"xbg{sl}"
            sAk, sBk, sCk = f"sA{sl}", f"sB{sl}", f"sC{sl}"
            sA, sB, sC = sAv(sl), sBv(sl), sCv(sl)
            for hh in range(2):
                E.op("g", lambda g, gk=gk, sl=sl, hh=hh: g.dma_start(
                    out=ab[:, sl * 4096 + hh * 2048:
                           sl * 4096 + (hh + 1) * 2048],
                    in_=xTg[:, gk * 4096 + hh * 2048:
                            gk * 4096 + (hh + 1) * 2048]),
                    writes=[xbk + "ab"[hh]], dma=True)
            for hh in range(2):
                E.op("s", lambda s, sl=sl, hh=hh: s.activation(
                    out=ab[:, 8192 + hh * 2048: 8192 + (hh + 1) * 2048],
                    in_=ab[:, sl * 4096 + hh * 2048:
                           sl * 4096 + (hh + 1) * 2048],
                    func=AF.Square),
                    reads=[xbk + "ab"[hh]], writes=["xsq" + "ab"[hh]])
            for hh in range(2):
                E.group("t", [
                    (lambda t, dc=dc, sl=sl: t.matmul(out=pbv[0],
                                                      lhsT=onesd[:],
                                                      rhs=xbg(sl, dc),
                                                      start=dc == 0,
                                                      stop=dc == 7))
                    for dc in range(4 * hh, 4 * hh + 4)],
                    reads=["onesd", xbk + "ab"[hh]], writes=["PB0"])
            for hh in range(2):
                E.group("t", [
                    (lambda t, dc=dc: t.matmul(out=pbv[1], lhsT=onesd[:],
                                               rhs=xsq(dc), start=dc == 0,
                                               stop=dc == 7))
                    for dc in range(4 * hh, 4 * hh + 4)],
                    reads=["onesd", "xsq" + "ab"[hh]], writes=["PB1"])

        def stats_tail(gk):
            sl = gk % 2
            sAk, sBk, sCk = f"sA{sl}", f"sB{sl}", f"sC{sl}"
            sA, sB, sC = sAv(sl), sBv(sl), sCv(sl)
            E.op("s", lambda s, sA=sA: s.copy(out=sA, in_=pbv[0]),
                 reads=["PB0"], writes=[sAk])
            E.op("v", lambda v, sA=sA, sB=sB: v.tensor_mul(out=sB, in0=sA,
                                                           in1=sA),
                 reads=[sAk], writes=[sBk])
            E.op("v", lambda v, sB=sB, sC=sC: v.tensor_sub(out=sC,
                                                           in0=pbv[1],
                                                           in1=sB),
                 reads=["PB1", sBk], writes=[sCk])
            E.op("s", lambda s, sB=sB, sC=sC: s.activation(out=sB, in_=sC,
                                                           func=AF.Ln,
                                                           bias=epsv[:]),
                 reads=[sCk, "epsv"], writes=[sBk])
            E.op("s", lambda s, sB=sB, sC=sC: s.activation(out=sC, in_=sB,
                                                           func=AF.Exp,
                                                           scale=-0.5),
                 reads=[sBk], writes=[sCk])

        def apply_part(gk):
            own = gk >= og0
            sl = gk % 2
            xbk = f"xbg{sl}"
            sAk, sCk = f"sA{sl}", f"sC{sl}"
            sA, sC = sAv(sl), sCv(sl)

            def dest_of(dc):
                if own:
                    return (xno(dc)[:, (gk - og0) * 512:(gk - og0 + 1) * 512],
                            f"xn{dc}")
                return xnt(dc), f"xt{dc}"

            for dc0 in range(0, 8, 2):
                for dc in (dc0, dc0 + 1):
                    sd, sdk = (sD0, "sD0") if dc % 2 == 0 else (sD1, "sD1")
                    E.op("v", lambda v, dc=dc, sl=sl, sA=sA, sd=sd:
                         v.tensor_sub(out=sd, in0=xbg(sl, dc), in1=sA),
                         reads=[xbk + ("a" if dc < 4 else "b"), sAk],
                         writes=[sdk])
                for dc in (dc0, dc0 + 1):
                    sd, sdk = (sD0, "sD0") if dc % 2 == 0 else (sD1, "sD1")
                    dest, dkey = dest_of(dc)
                    E.op("v", lambda v, dest=dest, sC=sC, sd=sd: v.tensor_mul(
                        out=dest, in0=sd, in1=sC),
                        reads=[sdk, sCk], writes=[dkey])

        def k_part(gk):
            own = gk >= og0
            src = (lambda dc, gk=gk: xno(dc)[:, (gk - og0) * 512:(gk - og0 + 1) * 512]) if own \
                else (lambda dc: xnt(dc))
            skeys = [f"xn{dc}" if own else f"xt{dc}" for dc in range(8)]
            for jc in range(8):
                bank = 2 + jc % 2
                E.group("t", [
                    (lambda t, dc=dc, jc=jc, bank=bank: t.matmul(
                        out=pbv[bank],
                        lhsT=wk_s[:, dc * 1024 + jc * 128: dc * 1024 + (jc + 1) * 128],
                        rhs=src(dc), start=dc == 0, stop=dc == 7))
                    for dc in range(8)],
                    reads=["wk"] + skeys, writes=[f"PB{bank}"])
                E.op("s", lambda s, jc=jc, bank=bank, gk=gk: s.activation(
                    out=kT(jc)[:, gk * 512:(gk + 1) * 512], in_=pbv[bank],
                    func=AF.Identity, bias=bias_s[:, 8 + jc: 9 + jc]),
                    reads=[f"PB{bank}", "biasp"], writes=[f"kT{jc}"])

        stats_part(0)
        stats_tail(0)
        for gk in range(kgn):
            if gk + 1 < kgn:
                stats_part(gk + 1)
            apply_part(gk)
            if gk + 1 < kgn:
                stats_tail(gk + 1)
            k_part(gk)
            emit_wload(6 if gk == 0 else 4)
        emit_wload(99)
        for dc in range(8):
            E.op("g", lambda g, dc=dc: g.dma_start(
                out=vr_s[:, dc * 32:(dc + 1) * 32],
                in_=vrd[dc * 128:(dc + 1) * 128, :]),
                writes=["vr"], dma=True)
        E.op("g", lambda g: g.dma_start(out=ct_s[:], in_=ctd[:]),
             writes=["ct"], dma=True)

        arena_src = (["xbg0a", "xbg0b", "xbg1a", "xbg1b", "xsqa", "xsqb",
                      "sD0", "sD1", "wst0", "wst1", "wst2", "wst3"]
                     + [f"sA{p}" for p in range(2)]
                     + [f"sB{p}" for p in range(2)]
                     + [f"sC{p}" for p in range(2)]
                     + [f"xt{dc}" for dc in range(8)])
        E.seal(["q0", "q1", "t1b", "g0", "g1", "xres0", "xres1",
                "out0", "out1", "c10", "c11", "rrow", "vs0", "vs1", "Rsb",
                "expo"],
               src_keys=arena_src)

        xnk = [f"xn{dc}" for dc in range(8)]

        def q_jit(ol, oslot):
            ocol = ol * 512
            qkey = f"q{oslot}"
            for jc in range(8):
                bank = 4 + jc % 2
                E.group("t", [
                    (lambda t, dc=dc, jc=jc, bank=bank, ocol=ocol: t.matmul(
                        out=pb[bank][:],
                        lhsT=wq_s[:, dc * 1024 + jc * 128: dc * 1024 + (jc + 1) * 128],
                        rhs=xno(dc)[:, ocol:ocol + 512],
                        start=dc == 0, stop=dc == 7))
                    for dc in range(8)],
                    reads=["wq"] + xnk, writes=[f"PB{bank}"])
                E.op("s", lambda s, jc=jc, bank=bank, oslot=oslot: s.activation(
                    out=qgrp(oslot, jc), in_=pb[bank][:],
                    func=AF.Identity, bias=bias_s[:, jc:jc + 1]),
                    reads=[f"PB{bank}", "biasp"], writes=[qkey])

        ccnt = [0]
        n_og = (hi - lo) // 4
        seq = list(reversed(range(n_og)))
        if len(seq) > 2:
            seq = [seq[0], seq[-1]] + seq[1:-1]
        q_jit(seq[0], 0)
        for si, ol in enumerate(seq):
            og = og0 + ol
            oslot = si % 2
            ocol = ol * 512
            qkey = f"q{oslot}"

            for i in range(4):
                t_ = og * 4 + i
                kext = 128 * (t_ + 1)
                nch = (kext + 1023) // 1024
                for c in range(nch):
                    w = 1024 if c < nch - 1 else kext - 1024 * (nch - 1)
                    par = ccnt[0] % 2
                    ccnt[0] += 1
                    big = pbBIG[par]
                    bkeys = [f"PB{2 * par}"] + \
                        ([f"PB{2 * par + 1}"] if w > 512 else [])
                    fns = []
                    for jc in range(8):
                        for hh in range(0, w, 512):
                            wh = min(512, w - hh)
                            fns.append(lambda t, jc=jc, c=c, i=i, hh=hh,
                                       wh=wh, big=big: t.matmul(
                                out=big[:, hh:hh + wh],
                                lhsT=qgrp(oslot, jc)[:, i * 128:(i + 1) * 128],
                                rhs=kT(jc)[:, c * 1024 + hh:
                                           c * 1024 + hh + wh],
                                start=jc == 0, stop=jc == 7))
                    E.group("t", fns,
                            reads=[qkey] + [f"kT{jc}" for jc in range(8)],
                            writes=bkeys)
                    if c == nch - 1:
                        m0 = w - 128
                        E.op("v", lambda v, big=big, m0=m0, w=w: v.tensor_mul(
                            out=big[:, m0:w], in0=big[:, m0:w],
                            in1=mask_s[:, 0:w - m0]),
                            reads=bkeys + ["maskp"], writes=bkeys)
                    ns, np_ = c % 2, (c + 1) % 2
                    E.op("s", lambda s, big=big, par=par, w=w: s.activation(
                        out=expo[:, 0:w], in_=big[:, 0:w], func=AF.Exp,
                        scale=INV_DK,
                        accum_out=csum[:, par:par + 1]),
                        reads=bkeys,
                        writes=["expo", f"cs{par}"])
                    if c == 0:
                        E.op("v", lambda v, par=par: v.tensor_copy(
                            out=sumb[:, 0:1], in_=csum[:, par:par + 1]),
                            reads=[f"cs{par}"], writes=["sum0"])
                    else:
                        E.op("v", lambda v, ns=ns, np_=np_, par=par:
                             v.tensor_add(
                                 out=sumb[:, ns:ns + 1],
                                 in0=sumb[:, np_:np_ + 1],
                                 in1=csum[:, par:par + 1]),
                             reads=[f"sum{np_}", f"cs{par}"],
                             writes=[f"sum{ns}"])
                last = (nch - 1) % 2
                zc = float(SEQ - kext)
                E.op("v", lambda v, zc=zc, last=last: v.tensor_scalar_add(
                    out=numb[:], in0=sumb[:, last:last + 1], scalar1=zc),
                    reads=[f"sum{last}"], writes=["num"])
                E.op("v", lambda v: v.reciprocal(out=zib[:], in_=numb[:]),
                     reads=["num"], writes=["zi"])
                E.op("v", lambda v, i=i: v.tensor_mul(
                    out=rgrp[:, i:i + 1], in0=numb[:], in1=zib[:]),
                    reads=["num", "zi"], writes=["rgrp"])
                if i == 2 and si + 1 < n_og:
                    q_jit(seq[si + 1], (si + 1) % 2)


            E.group("t", [
                (lambda t, dc=dc: t.matmul(
                    out=pb[5][0:32, :], lhsT=vr_s[:, dc * 32:(dc + 1) * 32],
                    rhs=xno(dc)[:, ocol:ocol + 512],
                    start=dc == 0, stop=dc == 7))
                for dc in range(8)], reads=["vr"] + xnk, writes=["PB5"])
            E.op("s", lambda s: s.copy(out=t1b[:], in_=pb[5][0:32, :]),
                 reads=["PB5"], writes=["t1b"])
            for oc in range(8):
                p = oc % 2
                wb = 6 if oc % 2 == 0 else 4
                vb = 7 if oc % 2 == 0 else 5
                E.op("y", lambda y, oc=oc, og=og: y.dma_start(
                    out=xres(oc % 2), in_=xTg[:, og * 4096 + oc * 512:
                                              og * 4096 + (oc + 1) * 512]),
                    writes=[f"xres{p}"], dma=True)
                fns = [
                    (lambda t, dc=dc, oc=oc, wb=wb: t.matmul(
                        out=pb[wb][:],
                        lhsT=wt_s[:, dc * 1024 + oc * 128: dc * 1024 + (oc + 1) * 128],
                        rhs=xno(dc)[:, ocol:ocol + 512],
                        start=dc == 0, stop=False))
                    for dc in range(8)]
                fns.append(lambda t, oc=oc, wb=wb: t.matmul(
                    out=pb[wb][:], lhsT=ct_s[:, oc * 128:(oc + 1) * 128],
                    rhs=t1b[:], start=False, stop=True))
                E.group("t", fns, reads=["wt", "ct", "t1b"] + xnk,
                        writes=[f"PB{wb}"])
                E.op("s", lambda s, oc=oc, wb=wb: s.activation(
                    out=grotb(oc % 2), in_=pb[wb][:], func=AF.Silu,
                    bias=bias_s[:, 24 + oc:25 + oc]),
                    reads=[f"PB{wb}", "biasp"], writes=[f"g{p}"])
                E.group("t", [
                    (lambda t, dc=dc, oc=oc, vb=vb: t.matmul(
                        out=pb[vb][:],
                        lhsT=wv_s[:, dc * 1024 + oc * 128: dc * 1024 + (oc + 1) * 128],
                        rhs=xno(dc)[:, ocol:ocol + 512],
                        start=dc == 0, stop=dc == 7))
                    for dc in range(8)], reads=["wv"] + xnk,
                    writes=[f"PB{vb}"])
                if oc == 0:
                    E.group("t", [
                        (lambda t, i=i: t.matmul(
                            out=pb[5][0:1, i * 128:(i + 1) * 128],
                            lhsT=rgrp[:, i:i + 1], rhs=ident_s[:],
                            start=True, stop=True))
                        for i in range(4)],
                        reads=["rgrp", "ident"], writes=["PB5"])
                    E.op("s", lambda s: s.copy(out=rrow[:], in_=pb[5][0:1, :]),
                         reads=["PB5"], writes=["rrow"])
                    E.op("t", lambda t: t.matmul(
                        out=pb[4][:], lhsT=ones1f[:], rhs=rrow[:],
                        start=True, stop=True),
                        reads=["ones1f", "rrow"], writes=["PB4"])
                    E.op("s", lambda s: s.copy(out=Rsb, in_=pb[4][:]),
                         reads=["PB4"], writes=["Rsb"])
                E.op("s", lambda s, oc=oc, vb=vb: s.activation(
                    out=vsb(oc % 2), in_=pb[vb][:], func=AF.Identity,
                    bias=bias_s[:, 16 + oc:17 + oc]),
                    reads=[f"PB{vb}", "biasp"], writes=[f"vs{p}"])
                E.op("g", lambda g, oc=oc: g.tensor_mul(
                    out=c1b(oc % 2), in0=vsb(oc % 2), in1=grotb(oc % 2)),
                    reads=[f"vs{p}", f"g{p}"], writes=[f"c1{p}"])
                E.op("g", lambda g, oc=oc: g.tensor_mul(
                    out=c1b(oc % 2), in0=c1b(oc % 2), in1=Rsb),
                    reads=[f"c1{p}", "Rsb"], writes=[f"c1{p}"])
                E.op("g", lambda g, oc=oc: g.tensor_add(
                    out=orot(oc % 2), in0=c1b(oc % 2), in1=xres(oc % 2)),
                    reads=[f"c1{p}", f"xres{p}"], writes=[f"out{p}"])
                E.op("y", lambda y, oc=oc, og=og: y.dma_start(
                    out=outT[oc * 128:(oc + 1) * 128,
                             og * 512:(og + 1) * 512],
                    in_=orot(oc % 2)),
                    reads=[f"out{p}"], dma=True)

        for i in range(Emitter.DMA_DEPTH):
            if E.cnt[f"dx{i}"]:
                nc.sync.wait_ge(sems[f"dx{i}"], E.cnt[f"dx{i}"])
            if E.cnt[f"dw{i}"]:
                nc.gpsimd.wait_ge(sems[f"dw{i}"], E.cnt[f"dw{i}"])

    if single_body is not None:
        emit_body(*((0, A_TILES) if single_body == 0 else (A_TILES, NT)))
    else:
        engines = [nc.tensor, nc.vector, nc.scalar, nc.sync, nc.gpsimd]
        idx = [e.partition_id() % 2 for e in engines]
        for b in nc.Switch(engines=engines, index=idx, n=2):
            if b == 0:
                emit_body(0, A_TILES)
            else:
                emit_body(A_TILES, NT)
    return nc


def _prep(X, Wq, bq, Wk, bk, Wv, bv, W, b, V_r, C):
    xTg = [np.ascontiguousarray(
        X[bb].reshape(8, 512, 8, 128).transpose(3, 0, 2, 1).reshape(128, 8 * SEQ))
        for bb in range(4)]
    biasp = np.empty((128, 32), dtype=np.float32)
    for wi, bb in enumerate((bq, bk, bv, b)):
        biasp[:, wi * 8:(wi + 1) * 8] = np.asarray(bb, np.float32).reshape(8, 128).T
    maskp = (np.arange(512)[None, :] <= np.arange(128)[:, None]).astype(np.float32)
    ident = np.eye(128, dtype=np.float32)
    shared = {
        "wq": np.ascontiguousarray(Wq, np.float32),
        "wk": np.ascontiguousarray(Wk, np.float32),
        "wv": np.ascontiguousarray(Wv, np.float32),
        "wt": np.ascontiguousarray(np.asarray(W, np.float32).T),
        "vr": np.ascontiguousarray(V_r, np.float32),
        "ct": np.ascontiguousarray(np.asarray(C, np.float32).T),
        "biasp": biasp, "maskp": maskp, "ident": ident,
    }
    return xTg, shared


def kernel(L0_attn_X, L0_attn_Wq, L0_attn_bq, L0_attn_Wk, L0_attn_bk,
           L0_attn_Wv, L0_attn_bv, L0_ffn_W, L0_ffn_b, L0_ffn_V_r, L0_ffn_C):
    X = np.asarray(L0_attn_X, np.float32)
    xTg, shared = _prep(X, L0_attn_Wq, L0_attn_bq, L0_attn_Wk, L0_attn_bk,
                        L0_attn_Wv, L0_attn_bv, L0_ffn_W, L0_ffn_b,
                        L0_ffn_V_r, L0_ffn_C)
    if "nc" not in _CACHE:
        _CACHE["nc"] = build_program()
    nc = _CACHE["nc"]
    in_maps = [dict(shared, xTg=xTg[c // 2]) for c in range(8)]
    res = run_bass_kernel_spmd(nc, in_maps, core_ids=list(range(8)))
    split = 128 * A_TILES
    out = np.empty((4, SEQ, D), dtype=np.float32)
    for bb in range(4):
        oa = np.asarray(res.results[2 * bb]["outT"])
        ob = np.asarray(res.results[2 * bb + 1]["outT"])
        out[bb, :split, :] = oa[:, :split].T
        out[bb, split:, :] = ob[:, split:].T
    return out


# version 23
# speedup vs baseline: 1.1454x; 1.1454x over previous
import math
import numpy as np
import ml_dtypes
import concourse.bass as bass
import concourse.mybir as mybir
from concourse.bass_utils import run_bass_kernel_spmd

F32 = mybir.dt.float32
BF16 = mybir.dt.bfloat16
AF = mybir.ActivationFunctionType
ALU = mybir.AluOpType
AX = mybir.AxisListType

A_TILES = 20
NT = 32
D = 1024
SEQ = 4096
INV_DK = 1.0 / 32.0


class Emitter:

    DMA_DEPTH = 6

    def __init__(self, nc, sems):
        self.nc = nc
        self.sems = sems
        self.cnt = {k: 0 for k in sems}
        self.lastw = {}
        self.readers = {}
        self.waited = {}
        self.dmasub = {"dx": 0, "dw": 0}
        self.eng = {
            "t": nc.tensor, "v": nc.vector, "s": nc.scalar,
            "y": nc.sync, "g": nc.gpsimd,
        }

    def _semname(self, e, dma):
        if e == "y":
            return "dx"
        if e == "g":
            return "dw" if dma else "gc"
        return {"t": "tt", "v": "vv", "s": "ss"}[e]

    def _wait(self, e, tok):
        sn, val = tok
        w = self.waited.setdefault(e, {})
        if w.get(sn, 0) >= val:
            return
        self.eng[e].wait_ge(self.sems[sn], val)
        w[sn] = val

    def _deps(self, e, reads, writes):
        toks = []
        for r in reads:
            t = self.lastw.get(r)
            if t:
                toks.append(t)
        for wkey in writes:
            t = self.lastw.get(wkey)
            if t:
                toks.append(t)
            toks.extend(self.readers.get(wkey, ()))
        best = {}
        for sn, val in toks:
            if best.get(sn, 0) < val:
                best[sn] = val
        for sn, val in best.items():
            self._wait(e, (sn, val))

    def _finish(self, e, inst, reads, writes, dma):
        sn = self._semname(e, dma)
        inc = 16 if dma else 1
        inst.then_inc(self.sems[sn], inc)
        self.cnt[sn] += inc
        tok = (sn, self.cnt[sn])
        for wkey in writes:
            self.lastw[wkey] = tok
            self.readers[wkey] = []
        for r in reads:
            self.readers.setdefault(r, []).append(tok)
        return tok

    def op(self, e, fn, reads=(), writes=(), dma=False):
        self._deps(e, reads, writes)
        if dma:
            q = self._semname(e, True)
            sn = f"{q}{self.dmasub[q]}"
            self.dmasub[q] = (self.dmasub[q] + 1) % self.DMA_DEPTH
            if self.cnt[sn] > 0:
                self._wait(e, (sn, self.cnt[sn]))
            inst = fn(self.eng[e])
            inst.then_inc(self.sems[sn], 16)
            self.cnt[sn] += 16
            tok = (sn, self.cnt[sn])
            for wkey in writes:
                self.lastw[wkey] = tok
                self.readers[wkey] = []
            for r in reads:
                self.readers.setdefault(r, []).append(tok)
            return tok
        inst = fn(self.eng[e])
        return self._finish(e, inst, reads, writes, dma)

    def group(self, e, fns, reads=(), writes=()):
        self._deps(e, reads, writes)
        inst = None
        for fn in fns:
            inst = fn(self.eng[e])
        return self._finish(e, inst, reads, writes, False)

    def seal(self, keys, src_keys=None):
        if src_keys is None:
            toks = [(sn, v) for sn, v in self.cnt.items() if v > 0]
        else:
            best = {}
            for k2 in src_keys:
                cand = list(self.readers.get(k2, ()))
                t = self.lastw.get(k2)
                if t:
                    cand.append(t)
                for sn, v in cand:
                    if best.get(sn, 0) < v:
                        best[sn] = v
            toks = list(best.items())
        for k in keys:
            self.readers[k] = list(toks)
            self.lastw.pop(k, None)


_CACHE = {}


def build_program(single_body=None):
    nc = bass.Bass("TRN2", target_bir_lowering=False)

    xTg = nc.dram_tensor("xTg", [128, 8 * SEQ], F32, kind="ExternalInput")
    xTb = nc.dram_tensor("xTb", [128, 8 * SEQ], BF16, kind="ExternalInput")
    wq = nc.dram_tensor("wq", [D, D], BF16, kind="ExternalInput")
    wk = nc.dram_tensor("wk", [D, D], BF16, kind="ExternalInput")
    wv = nc.dram_tensor("wv", [D, D], BF16, kind="ExternalInput")
    wt = nc.dram_tensor("wt", [D, D], BF16, kind="ExternalInput")
    vrd = nc.dram_tensor("vr", [D, 32], BF16, kind="ExternalInput")
    ctd = nc.dram_tensor("ct", [32, D], BF16, kind="ExternalInput")
    biasd = nc.dram_tensor("biasp", [128, 32], F32, kind="ExternalInput")
    maskd = nc.dram_tensor("maskp", [128, 512], BF16, kind="ExternalInput")
    identd = nc.dram_tensor("ident", [128, 128], F32, kind="ExternalInput")
    outT = nc.dram_tensor("outT", [D, SEQ], F32, kind="ExternalOutput")

    wq_s = nc.alloc_sbuf_tensor("wq_s", [128, 8192], BF16)
    wk_s = nc.alloc_sbuf_tensor("wk_s", [128, 8192], BF16)
    wv_s = nc.alloc_sbuf_tensor("wv_s", [128, 8192], BF16)
    wt_s = nc.alloc_sbuf_tensor("wt_s", [128, 8192], BF16)
    vr_s = nc.alloc_sbuf_tensor("vr_s", [128, 256], BF16)
    ct_s = nc.alloc_sbuf_tensor("ct_s", [32, 1024], BF16)
    bias_s = nc.alloc_sbuf_tensor("bias_s", [128, 32], F32)
    mask_s = nc.alloc_sbuf_tensor("mask_s", [128, 512], BF16)
    ident_s = nc.alloc_sbuf_tensor("ident_s", [128, 128], F32)
    onesd = nc.alloc_sbuf_tensor("onesd", [128, 128], BF16)
    ones1f = nc.alloc_sbuf_tensor("ones1f", [1, 128], F32)
    epsv = nc.alloc_sbuf_tensor("epsv", [128, 1], F32)
    rbig = nc.alloc_sbuf_tensor("rbig", [128, 45056], BF16)
    ab = nc.alloc_sbuf_tensor("ab", [128, 16384], BF16)
    af = nc.alloc_sbuf_tensor("af", [128, 4608], F32)
    cmax = nc.alloc_sbuf_tensor("cmax", [128, 1], F32)
    nmb = nc.alloc_sbuf_tensor("nmb", [128, 2], F32)
    sumb = nc.alloc_sbuf_tensor("sumb", [128, 2], F32)
    csum = nc.alloc_sbuf_tensor("csum", [128, 2], F32)
    dbuf = nc.alloc_sbuf_tensor("dbuf", [128, 1], F32)
    fbuf = nc.alloc_sbuf_tensor("fbuf", [128, 1], F32)
    ezb = nc.alloc_sbuf_tensor("ezb", [128, 1], F32)
    numb = nc.alloc_sbuf_tensor("numb", [128, 1], F32)
    zib = nc.alloc_sbuf_tensor("zib", [128, 1], F32)
    rgrp = nc.alloc_sbuf_tensor("rgrp", [128, 4], F32)

    pb01 = nc.alloc_psum_tensor("pb01", [128, 1024], F32)
    pb23 = nc.alloc_psum_tensor("pb23", [128, 1024], F32)
    pbv = {0: pb01[:, 0:512], 1: pb01[:, 512:1024],
           2: pb23[:, 0:512], 3: pb23[:, 512:1024]}
    pb = {i: nc.alloc_psum_tensor(f"pb{i}", [128, 512], F32)
          for i in range(4, 8)}
    pbBIG = [pb01, pb23]

    sems = {
        "tt": nc.alloc_semaphore("tt"), "vv": nc.alloc_semaphore("vv"),
        "ss": nc.alloc_semaphore("ss"), "gc": nc.alloc_semaphore("gc"),
    }
    for q in ("dx", "dw"):
        for i in range(Emitter.DMA_DEPTH):
            sems[f"{q}{i}"] = nc.alloc_semaphore(f"{q}{i}")

    def emit_body(lo, hi):
        E = Emitter(nc, sems)
        kgn = (hi + 3) // 4
        og0 = lo // 4
        kcols = kgn * 512
        ownlen = 128 * (hi - lo)

        def kT(jc):
            return rbig[:, jc * kcols:(jc + 1) * kcols]

        def xno(dc):
            base = 8 * kcols
            return rbig[:, base + dc * ownlen: base + (dc + 1) * ownlen]

        def xbg(slot, dc):
            return ab[:, slot * 4096 + dc * 512: slot * 4096 + (dc + 1) * 512]

        def xbg_all(slot):
            return ab[:, slot * 4096:(slot + 1) * 4096]

        def xsq(dc):
            return ab[:, 8192 + dc * 512: 8192 + (dc + 1) * 512]
        xsq_all = ab[:, 8192:12288]

        def xnt(dc):
            return ab[:, 12288 + dc * 512: 12288 + (dc + 1) * 512]

        def qgrp(slot, jc):
            return ab[:, slot * 4096 + jc * 512: slot * 4096 + (jc + 1) * 512]
        t1b = ab[0:32, 8192:8704]

        def grotb(p):
            return ab[:, 8704 + p * 512: 8704 + (p + 1) * 512]
        Rsb = ab[:, 10240:10752]
        expo = ab[:, 10752:11776]

        def sAv(sl):
            return af[:, sl * 1536: sl * 1536 + 512]

        def sBv(sl):
            return af[:, sl * 1536 + 512: sl * 1536 + 1024]

        def sCv(sl):
            return af[:, sl * 1536 + 1024: sl * 1536 + 1536]
        sD0 = af[:, 3072:3584]
        sD1 = af[:, 3584:4096]

        def wstage(p):
            return af[:, 2048 + p * 512: 2048 + (p + 1) * 512]

        def vsb(p):
            return af[:, p * 512:(p + 1) * 512]

        def xres(p):
            return af[:, 1024 + p * 512: 1024 + (p + 1) * 512]

        def orot(p):
            return af[:, 2048 + p * 512: 2048 + (p + 1) * 512]

        def c1b(p):
            return af[:, 3072 + p * 512: 3072 + (p + 1) * 512]
        rrow = af[0:1, 4096:4608]

        E.op("g", lambda g: g.dma_start(out=epsv[0:1, 0:1], in_=xTb[0:1, 0:1]),
             writes=["epsv"], dma=True)
        E.op("v", lambda v: v.memset(onesd[:], 1.0 / 1024.0), writes=["onesd"])
        E.op("v", lambda v: v.memset(ones1f[:], 1.0), writes=["ones1f"])
        E.op("v", lambda v: v.memset(epsv[:], 1e-5), writes=["epsv"])
        E.op("y", lambda y: y.dma_start(out=bias_s[:], in_=biasd[:]),
             writes=["biasp"], dma=True)
        E.op("y", lambda y: y.dma_start(out=ident_s[:], in_=identd[:]),
             writes=["ident"], dma=True)
        E.op("y", lambda y: y.dma_start(out=mask_s[:], in_=maskd[:]),
             writes=["maskp"], dma=True)
        for dc in range(8):
            E.op("y", lambda y, dc=dc: y.dma_start(
                out=wk_s[:, dc * 1024:(dc + 1) * 1024],
                in_=wk[dc * 128:(dc + 1) * 128, :]),
                writes=["wk"], dma=True)
        for dc in range(8):
            E.op("y", lambda y, dc=dc: y.dma_start(
                out=vr_s[:, dc * 32:(dc + 1) * 32],
                in_=vrd[dc * 128:(dc + 1) * 128, :]),
                writes=["vr"], dma=True)
        E.op("y", lambda y: y.dma_start(out=ct_s[:], in_=ctd[:]),
             writes=["ct"], dma=True)

        wload = []
        for nm, dram, sb in (("wq", wq, wq_s),
                             ("wv", wv, wv_s), ("wt", wt, wt_s)):
            for dc in range(8):
                wload.append((nm, dram, sb, dc))

        def emit_wload(n):
            while wload and n > 0:
                nm, dram, sb, dc = wload.pop(0)
                E.op("g", lambda g, dram=dram, sb=sb, dc=dc: g.dma_start(
                    out=sb[:, dc * 1024:(dc + 1) * 1024],
                    in_=dram[dc * 128:(dc + 1) * 128, :]),
                    writes=[nm], dma=True)
                n -= 1

        def stats_part(gk):
            sl = gk % 2
            xbk = f"xbg{sl}"
            sAk, sBk, sCk = f"sA{sl}", f"sB{sl}", f"sC{sl}"
            sA, sB, sC = sAv(sl), sBv(sl), sCv(sl)
            for hh in range(2):
                E.op("g", lambda g, gk=gk, sl=sl, hh=hh: g.dma_start(
                    out=ab[:, sl * 4096 + hh * 2048:
                           sl * 4096 + (hh + 1) * 2048],
                    in_=xTb[:, gk * 4096 + hh * 2048:
                            gk * 4096 + (hh + 1) * 2048]),
                    writes=[xbk + "ab"[hh]], dma=True)
            for hh in range(2):
                E.op("s", lambda s, sl=sl, hh=hh: s.activation(
                    out=ab[:, 8192 + hh * 2048: 8192 + (hh + 1) * 2048],
                    in_=ab[:, sl * 4096 + hh * 2048:
                           sl * 4096 + (hh + 1) * 2048],
                    func=AF.Square),
                    reads=[xbk + "ab"[hh]], writes=["xsq" + "ab"[hh]])
            for hh in range(2):
                E.group("t", [
                    (lambda t, dc=dc, sl=sl: t.matmul(out=pbv[0],
                                                      lhsT=onesd[:],
                                                      rhs=xbg(sl, dc),
                                                      start=dc == 0,
                                                      stop=dc == 7))
                    for dc in range(4 * hh, 4 * hh + 4)],
                    reads=["onesd", xbk + "ab"[hh]], writes=["PB0"])
            for hh in range(2):
                E.group("t", [
                    (lambda t, dc=dc: t.matmul(out=pbv[1], lhsT=onesd[:],
                                               rhs=xsq(dc), start=dc == 0,
                                               stop=dc == 7))
                    for dc in range(4 * hh, 4 * hh + 4)],
                    reads=["onesd", "xsq" + "ab"[hh]], writes=["PB1"])

        def stats_tail(gk):
            sl = gk % 2
            sAk, sBk, sCk = f"sA{sl}", f"sB{sl}", f"sC{sl}"
            sA, sB, sC = sAv(sl), sBv(sl), sCv(sl)
            E.op("s", lambda s, sA=sA: s.copy(out=sA, in_=pbv[0]),
                 reads=["PB0"], writes=[sAk])
            E.op("v", lambda v, sA=sA, sB=sB: v.tensor_mul(out=sB, in0=sA,
                                                           in1=sA),
                 reads=[sAk], writes=[sBk])
            E.op("v", lambda v, sB=sB, sC=sC: v.tensor_sub(out=sC,
                                                           in0=pbv[1],
                                                           in1=sB),
                 reads=["PB1", sBk], writes=[sCk])
            E.op("s", lambda s, sB=sB, sC=sC: s.activation(out=sB, in_=sC,
                                                           func=AF.Ln,
                                                           bias=epsv[:]),
                 reads=[sCk, "epsv"], writes=[sBk])
            E.op("s", lambda s, sB=sB, sC=sC: s.activation(out=sC, in_=sB,
                                                           func=AF.Exp,
                                                           scale=-0.5),
                 reads=[sBk], writes=[sCk])

        def apply_part(gk):
            own = gk >= og0
            sl = gk % 2
            xbk = f"xbg{sl}"
            sAk, sCk = f"sA{sl}", f"sC{sl}"
            sA, sC = sAv(sl), sCv(sl)

            def dest_of(dc):
                if own:
                    return (xno(dc)[:, (gk - og0) * 512:(gk - og0 + 1) * 512],
                            f"xn{dc}")
                return xnt(dc), f"xt{dc}"

            for dc0 in range(0, 8, 2):
                for dc in (dc0, dc0 + 1):
                    sd, sdk = (sD0, "sD0") if dc % 2 == 0 else (sD1, "sD1")
                    E.op("v", lambda v, dc=dc, sl=sl, sA=sA, sd=sd:
                         v.tensor_sub(out=sd, in0=xbg(sl, dc), in1=sA),
                         reads=[xbk + ("a" if dc < 4 else "b"), sAk],
                         writes=[sdk])
                for dc in (dc0, dc0 + 1):
                    sd, sdk = (sD0, "sD0") if dc % 2 == 0 else (sD1, "sD1")
                    dest, dkey = dest_of(dc)
                    E.op("v", lambda v, dest=dest, sC=sC, sd=sd: v.tensor_mul(
                        out=dest, in0=sd, in1=sC),
                        reads=[sdk, sCk], writes=[dkey])

        def k_part(gk):
            own = gk >= og0
            src = (lambda dc, gk=gk: xno(dc)[:, (gk - og0) * 512:(gk - og0 + 1) * 512]) if own \
                else (lambda dc: xnt(dc))
            skeys = [f"xn{dc}" if own else f"xt{dc}" for dc in range(8)]
            for jc in range(8):
                bank = 2 + jc % 2
                E.group("t", [
                    (lambda t, dc=dc, jc=jc, bank=bank: t.matmul(
                        out=pbv[bank],
                        lhsT=wk_s[:, dc * 1024 + jc * 128: dc * 1024 + (jc + 1) * 128],
                        rhs=src(dc), start=dc == 0, stop=dc == 7))
                    for dc in range(8)],
                    reads=["wk"] + skeys, writes=[f"PB{bank}"])
                E.op("s", lambda s, jc=jc, bank=bank, gk=gk: s.activation(
                    out=kT(jc)[:, gk * 512:(gk + 1) * 512], in_=pbv[bank],
                    func=AF.Identity, bias=bias_s[:, 8 + jc: 9 + jc]),
                    reads=[f"PB{bank}", "biasp"], writes=[f"kT{jc}"])

        stats_part(0)
        stats_tail(0)
        for gk in range(kgn):
            if gk + 1 < kgn:
                stats_part(gk + 1)
            apply_part(gk)
            if gk + 1 < kgn:
                stats_tail(gk + 1)
            k_part(gk)
            emit_wload(6 if gk == 0 else 4)
        emit_wload(99)

        arena_src = (["xbg0a", "xbg0b", "xbg1a", "xbg1b", "xsqa", "xsqb",
                      "sD0", "sD1", "wst0", "wst1", "wst2", "wst3"]
                     + [f"sA{p}" for p in range(2)]
                     + [f"sB{p}" for p in range(2)]
                     + [f"sC{p}" for p in range(2)]
                     + [f"xt{dc}" for dc in range(8)])
        E.seal(["q0", "q1", "t1b", "g0", "g1", "xres0", "xres1",
                "out0", "out1", "c10", "c11", "rrow", "vs0", "vs1", "Rsb",
                "expo"],
               src_keys=arena_src)

        xnk = [f"xn{dc}" for dc in range(8)]

        def q_jit(ol, oslot):
            ocol = ol * 512
            qkey = f"q{oslot}"
            for jc in range(8):
                bank = 4 + jc % 2
                E.group("t", [
                    (lambda t, dc=dc, jc=jc, bank=bank, ocol=ocol: t.matmul(
                        out=pb[bank][:],
                        lhsT=wq_s[:, dc * 1024 + jc * 128: dc * 1024 + (jc + 1) * 128],
                        rhs=xno(dc)[:, ocol:ocol + 512],
                        start=dc == 0, stop=dc == 7))
                    for dc in range(8)],
                    reads=["wq"] + xnk, writes=[f"PB{bank}"])
                E.op("s", lambda s, jc=jc, bank=bank, oslot=oslot: s.activation(
                    out=qgrp(oslot, jc), in_=pb[bank][:],
                    func=AF.Identity, bias=bias_s[:, jc:jc + 1]),
                    reads=[f"PB{bank}", "biasp"], writes=[qkey])

        ccnt = [0]
        n_og = (hi - lo) // 4
        seq = list(reversed(range(n_og)))
        if len(seq) > 2:
            seq = [seq[0], seq[-1]] + seq[1:-1]
        q_jit(seq[0], 0)
        for si, ol in enumerate(seq):
            og = og0 + ol
            oslot = si % 2
            ocol = ol * 512
            qkey = f"q{oslot}"

            for i in range(4):
                t_ = og * 4 + i
                kext = 128 * (t_ + 1)
                nch = (kext + 1023) // 1024
                for c in range(nch):
                    w = 1024 if c < nch - 1 else kext - 1024 * (nch - 1)
                    par = ccnt[0] % 2
                    ccnt[0] += 1
                    big = pbBIG[par]
                    bkeys = [f"PB{2 * par}"] + \
                        ([f"PB{2 * par + 1}"] if w > 512 else [])
                    fns = []
                    for jc in range(8):
                        for hh in range(0, w, 512):
                            wh = min(512, w - hh)
                            fns.append(lambda t, jc=jc, c=c, i=i, hh=hh,
                                       wh=wh, big=big: t.matmul(
                                out=big[:, hh:hh + wh],
                                lhsT=qgrp(oslot, jc)[:, i * 128:(i + 1) * 128],
                                rhs=kT(jc)[:, c * 1024 + hh:
                                           c * 1024 + hh + wh],
                                start=jc == 0, stop=jc == 7))
                    E.group("t", fns,
                            reads=[qkey] + [f"kT{jc}" for jc in range(8)],
                            writes=bkeys)
                    if c == nch - 1:
                        m0 = w - 128
                        E.op("v", lambda v, big=big, m0=m0, w=w: v.tensor_mul(
                            out=big[:, m0:w], in0=big[:, m0:w],
                            in1=mask_s[:, 0:w - m0]),
                            reads=bkeys + ["maskp"], writes=bkeys)
                    ns, np_ = c % 2, (c + 1) % 2
                    E.op("s", lambda s, big=big, par=par, w=w: s.activation(
                        out=expo[:, 0:w], in_=big[:, 0:w], func=AF.Exp,
                        scale=INV_DK,
                        accum_out=csum[:, par:par + 1]),
                        reads=bkeys,
                        writes=["expo", f"cs{par}"])
                    if c == 0:
                        E.op("v", lambda v, par=par: v.tensor_copy(
                            out=sumb[:, 0:1], in_=csum[:, par:par + 1]),
                            reads=[f"cs{par}"], writes=["sum0"])
                    else:
                        E.op("v", lambda v, ns=ns, np_=np_, par=par:
                             v.tensor_add(
                                 out=sumb[:, ns:ns + 1],
                                 in0=sumb[:, np_:np_ + 1],
                                 in1=csum[:, par:par + 1]),
                             reads=[f"sum{np_}", f"cs{par}"],
                             writes=[f"sum{ns}"])
                last = (nch - 1) % 2
                zc = float(SEQ - kext)
                E.op("v", lambda v, zc=zc, last=last: v.tensor_scalar_add(
                    out=numb[:], in0=sumb[:, last:last + 1], scalar1=zc),
                    reads=[f"sum{last}"], writes=["num"])
                E.op("v", lambda v: v.reciprocal(out=zib[:], in_=numb[:]),
                     reads=["num"], writes=["zi"])
                E.op("v", lambda v, i=i: v.tensor_mul(
                    out=rgrp[:, i:i + 1], in0=numb[:], in1=zib[:]),
                    reads=["num", "zi"], writes=["rgrp"])
                if i == 2 and si + 1 < n_og:
                    q_jit(seq[si + 1], (si + 1) % 2)


            E.group("t", [
                (lambda t, dc=dc: t.matmul(
                    out=pb[5][0:32, :], lhsT=vr_s[:, dc * 32:(dc + 1) * 32],
                    rhs=xno(dc)[:, ocol:ocol + 512],
                    start=dc == 0, stop=dc == 7))
                for dc in range(8)], reads=["vr"] + xnk, writes=["PB5"])
            E.op("s", lambda s: s.copy(out=t1b[:], in_=pb[5][0:32, :]),
                 reads=["PB5"], writes=["t1b"])
            for oc in range(8):
                p = oc % 2
                wb = 6 if oc % 2 == 0 else 4
                vb = 7 if oc % 2 == 0 else 5
                E.op("y", lambda y, oc=oc, og=og: y.dma_start(
                    out=xres(oc % 2), in_=xTg[:, og * 4096 + oc * 512:
                                              og * 4096 + (oc + 1) * 512]),
                    writes=[f"xres{p}"], dma=True)
                fns = [
                    (lambda t, dc=dc, oc=oc, wb=wb: t.matmul(
                        out=pb[wb][:],
                        lhsT=wt_s[:, dc * 1024 + oc * 128: dc * 1024 + (oc + 1) * 128],
                        rhs=xno(dc)[:, ocol:ocol + 512],
                        start=dc == 0, stop=False))
                    for dc in range(8)]
                fns.append(lambda t, oc=oc, wb=wb: t.matmul(
                    out=pb[wb][:], lhsT=ct_s[:, oc * 128:(oc + 1) * 128],
                    rhs=t1b[:], start=False, stop=True))
                E.group("t", fns, reads=["wt", "ct", "t1b"] + xnk,
                        writes=[f"PB{wb}"])
                E.op("s", lambda s, oc=oc, wb=wb: s.activation(
                    out=grotb(oc % 2), in_=pb[wb][:], func=AF.Silu,
                    bias=bias_s[:, 24 + oc:25 + oc]),
                    reads=[f"PB{wb}", "biasp"], writes=[f"g{p}"])
                E.group("t", [
                    (lambda t, dc=dc, oc=oc, vb=vb: t.matmul(
                        out=pb[vb][:],
                        lhsT=wv_s[:, dc * 1024 + oc * 128: dc * 1024 + (oc + 1) * 128],
                        rhs=xno(dc)[:, ocol:ocol + 512],
                        start=dc == 0, stop=dc == 7))
                    for dc in range(8)], reads=["wv"] + xnk,
                    writes=[f"PB{vb}"])
                if oc == 0:
                    E.group("t", [
                        (lambda t, i=i: t.matmul(
                            out=pb[5][0:1, i * 128:(i + 1) * 128],
                            lhsT=rgrp[:, i:i + 1], rhs=ident_s[:],
                            start=True, stop=True))
                        for i in range(4)],
                        reads=["rgrp", "ident"], writes=["PB5"])
                    E.op("s", lambda s: s.copy(out=rrow[:], in_=pb[5][0:1, :]),
                         reads=["PB5"], writes=["rrow"])
                    E.op("t", lambda t: t.matmul(
                        out=pb[4][:], lhsT=ones1f[:], rhs=rrow[:],
                        start=True, stop=True),
                        reads=["ones1f", "rrow"], writes=["PB4"])
                    E.op("s", lambda s: s.copy(out=Rsb, in_=pb[4][:]),
                         reads=["PB4"], writes=["Rsb"])
                E.op("s", lambda s, oc=oc, vb=vb: s.activation(
                    out=vsb(oc % 2), in_=pb[vb][:], func=AF.Identity,
                    bias=bias_s[:, 16 + oc:17 + oc]),
                    reads=[f"PB{vb}", "biasp"], writes=[f"vs{p}"])
                ee = "g" if oc % 2 == 0 else "v"
                E.op(ee, lambda e, oc=oc: e.tensor_mul(
                    out=c1b(oc % 2), in0=vsb(oc % 2), in1=grotb(oc % 2)),
                    reads=[f"vs{p}", f"g{p}"], writes=[f"c1{p}"])
                E.op(ee, lambda e, oc=oc: e.tensor_mul(
                    out=c1b(oc % 2), in0=c1b(oc % 2), in1=Rsb),
                    reads=[f"c1{p}", "Rsb"], writes=[f"c1{p}"])
                E.op(ee, lambda e, oc=oc: e.tensor_add(
                    out=orot(oc % 2), in0=c1b(oc % 2), in1=xres(oc % 2)),
                    reads=[f"c1{p}", f"xres{p}"], writes=[f"out{p}"])
                E.op("y", lambda y, oc=oc, og=og: y.dma_start(
                    out=outT[oc * 128:(oc + 1) * 128,
                             og * 512:(og + 1) * 512],
                    in_=orot(oc % 2)),
                    reads=[f"out{p}"], dma=True)

        for i in range(Emitter.DMA_DEPTH):
            if E.cnt[f"dx{i}"]:
                nc.sync.wait_ge(sems[f"dx{i}"], E.cnt[f"dx{i}"])
            if E.cnt[f"dw{i}"]:
                nc.gpsimd.wait_ge(sems[f"dw{i}"], E.cnt[f"dw{i}"])

    if single_body is not None:
        emit_body(*((0, A_TILES) if single_body == 0 else (A_TILES, NT)))
    else:
        engines = [nc.tensor, nc.vector, nc.scalar, nc.sync, nc.gpsimd]
        idx = [e.partition_id() % 2 for e in engines]
        for b in nc.Switch(engines=engines, index=idx, n=2):
            if b == 0:
                emit_body(0, A_TILES)
            else:
                emit_body(A_TILES, NT)
    return nc


def _prep(X, Wq, bq, Wk, bk, Wv, bv, W, b, V_r, C):
    BF = ml_dtypes.bfloat16
    xTg = [np.ascontiguousarray(
        X[bb].reshape(8, 512, 8, 128).transpose(3, 0, 2, 1).reshape(128, 8 * SEQ))
        for bb in range(4)]
    xTb = [t.astype(BF) for t in xTg]
    biasp = np.empty((128, 32), dtype=np.float32)
    for wi, bb in enumerate((bq, bk, bv, b)):
        biasp[:, wi * 8:(wi + 1) * 8] = np.asarray(bb, np.float32).reshape(8, 128).T
    maskp = (np.arange(512)[None, :] <= np.arange(128)[:, None]).astype(BF)
    ident = np.eye(128, dtype=np.float32)
    shared = {
        "wq": np.ascontiguousarray(np.asarray(Wq, np.float32).astype(BF)),
        "wk": np.ascontiguousarray(np.asarray(Wk, np.float32).astype(BF)),
        "wv": np.ascontiguousarray(np.asarray(Wv, np.float32).astype(BF)),
        "wt": np.ascontiguousarray(np.asarray(W, np.float32).T.astype(BF)),
        "vr": np.ascontiguousarray(np.asarray(V_r, np.float32).astype(BF)),
        "ct": np.ascontiguousarray(np.asarray(C, np.float32).T.astype(BF)),
        "biasp": biasp, "maskp": maskp, "ident": ident,
    }
    return xTg, xTb, shared


def kernel(L0_attn_X, L0_attn_Wq, L0_attn_bq, L0_attn_Wk, L0_attn_bk,
           L0_attn_Wv, L0_attn_bv, L0_ffn_W, L0_ffn_b, L0_ffn_V_r, L0_ffn_C):
    X = np.asarray(L0_attn_X, np.float32)
    xTg, xTb, shared = _prep(X, L0_attn_Wq, L0_attn_bq, L0_attn_Wk,
                             L0_attn_bk, L0_attn_Wv, L0_attn_bv, L0_ffn_W,
                             L0_ffn_b, L0_ffn_V_r, L0_ffn_C)
    if "nc" not in _CACHE:
        _CACHE["nc"] = build_program()
    nc = _CACHE["nc"]
    in_maps = [dict(shared, xTg=xTg[c // 2], xTb=xTb[c // 2])
               for c in range(8)]
    res = run_bass_kernel_spmd(nc, in_maps, core_ids=list(range(8)))
    split = 128 * A_TILES
    out = np.empty((4, SEQ, D), dtype=np.float32)
    for bb in range(4):
        oa = np.asarray(res.results[2 * bb]["outT"])
        ob = np.asarray(res.results[2 * bb + 1]["outT"])
        out[bb, :split, :] = oa[:, :split].T
        out[bb, split:, :] = ob[:, split:].T
    return out


# version 24
# speedup vs baseline: 1.2060x; 1.0529x over previous
import math
import numpy as np
import ml_dtypes
import concourse.bass as bass
import concourse.mybir as mybir
from concourse.bass_utils import run_bass_kernel_spmd

F32 = mybir.dt.float32
BF16 = mybir.dt.bfloat16
AF = mybir.ActivationFunctionType
ALU = mybir.AluOpType
AX = mybir.AxisListType

A_TILES = 20
NT = 32
D = 1024
SEQ = 4096
INV_DK = 1.0 / 32.0


class Emitter:

    DMA_DEPTH = 6

    def __init__(self, nc, sems):
        self.nc = nc
        self.sems = sems
        self.cnt = {k: 0 for k in sems}
        self.lastw = {}
        self.readers = {}
        self.waited = {}
        self.dmasub = {"dx": 0, "dw": 0}
        self.eng = {
            "t": nc.tensor, "v": nc.vector, "s": nc.scalar,
            "y": nc.sync, "g": nc.gpsimd,
        }

    def _semname(self, e, dma):
        if e == "y":
            return "dx"
        if e == "g":
            return "dw" if dma else "gc"
        return {"t": "tt", "v": "vv", "s": "ss"}[e]

    def _wait(self, e, tok):
        sn, val = tok
        w = self.waited.setdefault(e, {})
        if w.get(sn, 0) >= val:
            return
        self.eng[e].wait_ge(self.sems[sn], val)
        w[sn] = val

    def _deps(self, e, reads, writes):
        toks = []
        for r in reads:
            t = self.lastw.get(r)
            if t:
                toks.append(t)
        for wkey in writes:
            t = self.lastw.get(wkey)
            if t:
                toks.append(t)
            toks.extend(self.readers.get(wkey, ()))
        best = {}
        for sn, val in toks:
            if best.get(sn, 0) < val:
                best[sn] = val
        for sn, val in best.items():
            self._wait(e, (sn, val))

    def _finish(self, e, inst, reads, writes, dma):
        sn = self._semname(e, dma)
        inc = 16 if dma else 1
        inst.then_inc(self.sems[sn], inc)
        self.cnt[sn] += inc
        tok = (sn, self.cnt[sn])
        for wkey in writes:
            self.lastw[wkey] = tok
            self.readers[wkey] = []
        for r in reads:
            self.readers.setdefault(r, []).append(tok)
        return tok

    def op(self, e, fn, reads=(), writes=(), dma=False):
        self._deps(e, reads, writes)
        if dma:
            q = self._semname(e, True)
            sn = f"{q}{self.dmasub[q]}"
            self.dmasub[q] = (self.dmasub[q] + 1) % self.DMA_DEPTH
            if self.cnt[sn] > 0:
                self._wait(e, (sn, self.cnt[sn]))
            inst = fn(self.eng[e])
            inst.then_inc(self.sems[sn], 16)
            self.cnt[sn] += 16
            tok = (sn, self.cnt[sn])
            for wkey in writes:
                self.lastw[wkey] = tok
                self.readers[wkey] = []
            for r in reads:
                self.readers.setdefault(r, []).append(tok)
            return tok
        inst = fn(self.eng[e])
        return self._finish(e, inst, reads, writes, dma)

    def group(self, e, fns, reads=(), writes=()):
        self._deps(e, reads, writes)
        inst = None
        for fn in fns:
            inst = fn(self.eng[e])
        return self._finish(e, inst, reads, writes, False)

    def seal(self, keys, src_keys=None):
        if src_keys is None:
            toks = [(sn, v) for sn, v in self.cnt.items() if v > 0]
        else:
            best = {}
            for k2 in src_keys:
                cand = list(self.readers.get(k2, ()))
                t = self.lastw.get(k2)
                if t:
                    cand.append(t)
                for sn, v in cand:
                    if best.get(sn, 0) < v:
                        best[sn] = v
            toks = list(best.items())
        for k in keys:
            self.readers[k] = list(toks)
            self.lastw.pop(k, None)


_CACHE = {}


def build_program(single_body=None):
    nc = bass.Bass("TRN2", target_bir_lowering=False)

    xTg = nc.dram_tensor("xTg", [128, 8 * SEQ], F32, kind="ExternalInput")
    xTb = nc.dram_tensor("xTb", [128, 8 * SEQ], BF16, kind="ExternalInput")
    wq = nc.dram_tensor("wq", [D, D], BF16, kind="ExternalInput")
    wk = nc.dram_tensor("wk", [D, D], BF16, kind="ExternalInput")
    wv = nc.dram_tensor("wv", [D, D], BF16, kind="ExternalInput")
    wt = nc.dram_tensor("wt", [D, D], BF16, kind="ExternalInput")
    vrd = nc.dram_tensor("vr", [D, 32], BF16, kind="ExternalInput")
    ctd = nc.dram_tensor("ct", [32, D], BF16, kind="ExternalInput")
    biasd = nc.dram_tensor("biasp", [128, 32], F32, kind="ExternalInput")
    maskd = nc.dram_tensor("maskp", [128, 512], BF16, kind="ExternalInput")
    identd = nc.dram_tensor("ident", [128, 128], F32, kind="ExternalInput")
    outT = nc.dram_tensor("outT", [D, SEQ], F32, kind="ExternalOutput")

    wq_s = nc.alloc_sbuf_tensor("wq_s", [128, 8192], BF16)
    wk_s = nc.alloc_sbuf_tensor("wk_s", [128, 8192], BF16)
    wv_s = nc.alloc_sbuf_tensor("wv_s", [128, 8192], BF16)
    wt_s = nc.alloc_sbuf_tensor("wt_s", [128, 8192], BF16)
    vr_s = nc.alloc_sbuf_tensor("vr_s", [128, 256], BF16)
    ct_s = nc.alloc_sbuf_tensor("ct_s", [32, 1024], BF16)
    bias_s = nc.alloc_sbuf_tensor("bias_s", [128, 32], F32)
    mask_s = nc.alloc_sbuf_tensor("mask_s", [128, 512], BF16)
    ident_s = nc.alloc_sbuf_tensor("ident_s", [128, 128], F32)
    onesd = nc.alloc_sbuf_tensor("onesd", [128, 128], BF16)
    ones1f = nc.alloc_sbuf_tensor("ones1f", [1, 128], F32)
    epsv = nc.alloc_sbuf_tensor("epsv", [128, 1], F32)
    rbig = nc.alloc_sbuf_tensor("rbig", [128, 45056], BF16)
    ab = nc.alloc_sbuf_tensor("ab", [128, 16384], BF16)
    af = nc.alloc_sbuf_tensor("af", [128, 4608], F32)
    cmax = nc.alloc_sbuf_tensor("cmax", [128, 1], F32)
    nmb = nc.alloc_sbuf_tensor("nmb", [128, 2], F32)
    sumb = nc.alloc_sbuf_tensor("sumb", [128, 2], F32)
    csum = nc.alloc_sbuf_tensor("csum", [128, 2], F32)
    dbuf = nc.alloc_sbuf_tensor("dbuf", [128, 1], F32)
    fbuf = nc.alloc_sbuf_tensor("fbuf", [128, 1], F32)
    ezb = nc.alloc_sbuf_tensor("ezb", [128, 1], F32)
    numb = nc.alloc_sbuf_tensor("numb", [128, 1], F32)
    zib = nc.alloc_sbuf_tensor("zib", [128, 1], F32)
    rgrp = nc.alloc_sbuf_tensor("rgrp", [128, 4], F32)

    pb01 = nc.alloc_psum_tensor("pb01", [128, 1024], F32)
    pb23 = nc.alloc_psum_tensor("pb23", [128, 1024], F32)
    pbv = {0: pb01[:, 0:512], 1: pb01[:, 512:1024],
           2: pb23[:, 0:512], 3: pb23[:, 512:1024]}
    pb = {i: nc.alloc_psum_tensor(f"pb{i}", [128, 512], F32)
          for i in range(4, 8)}
    pbBIG = [pb01, pb23]

    sems = {
        "tt": nc.alloc_semaphore("tt"), "vv": nc.alloc_semaphore("vv"),
        "ss": nc.alloc_semaphore("ss"), "gc": nc.alloc_semaphore("gc"),
    }
    for q in ("dx", "dw"):
        for i in range(Emitter.DMA_DEPTH):
            sems[f"{q}{i}"] = nc.alloc_semaphore(f"{q}{i}")

    def emit_body(lo, hi):
        E = Emitter(nc, sems)
        kgn = (hi + 3) // 4
        og0 = lo // 4
        kcols = kgn * 512
        ownlen = 128 * (hi - lo)

        def kT(jc):
            return rbig[:, jc * kcols:(jc + 1) * kcols]

        def xno(dc):
            base = 8 * kcols
            return rbig[:, base + dc * ownlen: base + (dc + 1) * ownlen]

        def xbg(slot, dc):
            return ab[:, slot * 4096 + dc * 512: slot * 4096 + (dc + 1) * 512]

        def xbg_all(slot):
            return ab[:, slot * 4096:(slot + 1) * 4096]

        def xsq(dc):
            return ab[:, 8192 + dc * 512: 8192 + (dc + 1) * 512]
        xsq_all = ab[:, 8192:12288]

        def xnt(dc):
            return ab[:, 12288 + dc * 512: 12288 + (dc + 1) * 512]

        def qgrp(slot, jc):
            return ab[:, slot * 4096 + jc * 512: slot * 4096 + (jc + 1) * 512]
        t1b = ab[0:32, 8192:8704]

        def grotb(p):
            return ab[:, 8704 + p * 512: 8704 + (p + 1) * 512]
        Rsb = ab[:, 10240:10752]
        expo = ab[:, 10752:11776]

        def sAv(sl):
            return af[:, sl * 1536: sl * 1536 + 512]

        def sBv(sl):
            return af[:, sl * 1536 + 512: sl * 1536 + 1024]

        def sCv(sl):
            return af[:, sl * 1536 + 1024: sl * 1536 + 1536]
        sD0 = af[:, 3072:3584]
        sD1 = af[:, 3584:4096]

        def wstage(p):
            return af[:, 2048 + p * 512: 2048 + (p + 1) * 512]

        def vsb(p):
            return af[:, p * 512:(p + 1) * 512]

        def xres(p):
            return af[:, 1024 + p * 512: 1024 + (p + 1) * 512]

        def orot(p):
            return af[:, 2048 + p * 512: 2048 + (p + 1) * 512]

        def c1b(p):
            return af[:, 3072 + p * 512: 3072 + (p + 1) * 512]
        rrow = af[0:1, 4096:4608]

        E.op("g", lambda g: g.dma_start(out=epsv[0:1, 0:1], in_=xTb[0:1, 0:1]),
             writes=["epsv"], dma=True)
        E.op("v", lambda v: v.memset(onesd[:], 1.0 / 1024.0), writes=["onesd"])
        E.op("v", lambda v: v.memset(ones1f[:], 1.0), writes=["ones1f"])
        E.op("v", lambda v: v.memset(epsv[:], 1e-5), writes=["epsv"])
        E.op("y", lambda y: y.dma_start(out=bias_s[:], in_=biasd[:]),
             writes=["biasp"], dma=True)
        E.op("y", lambda y: y.dma_start(out=ident_s[:], in_=identd[:]),
             writes=["ident"], dma=True)
        E.op("y", lambda y: y.dma_start(out=mask_s[:], in_=maskd[:]),
             writes=["maskp"], dma=True)
        for dc in range(8):
            E.op("y", lambda y, dc=dc: y.dma_start(
                out=wk_s[:, dc * 1024:(dc + 1) * 1024],
                in_=wk[dc * 128:(dc + 1) * 128, :]),
                writes=["wk"], dma=True)
        for dc in range(8):
            E.op("y", lambda y, dc=dc: y.dma_start(
                out=vr_s[:, dc * 32:(dc + 1) * 32],
                in_=vrd[dc * 128:(dc + 1) * 128, :]),
                writes=["vr"], dma=True)
        E.op("y", lambda y: y.dma_start(out=ct_s[:], in_=ctd[:]),
             writes=["ct"], dma=True)

        wload = []
        for nm, dram, sb in (("wq", wq, wq_s),
                             ("wv", wv, wv_s), ("wt", wt, wt_s)):
            for dc in range(8):
                wload.append((nm, dram, sb, dc))

        def emit_wload(n):
            while wload and n > 0:
                nm, dram, sb, dc = wload.pop(0)
                E.op("g", lambda g, dram=dram, sb=sb, dc=dc: g.dma_start(
                    out=sb[:, dc * 1024:(dc + 1) * 1024],
                    in_=dram[dc * 128:(dc + 1) * 128, :]),
                    writes=[nm], dma=True)
                n -= 1

        def stats_part(gk):
            sl = gk % 2
            xbk = f"xbg{sl}"
            sAk, sBk, sCk = f"sA{sl}", f"sB{sl}", f"sC{sl}"
            sA, sB, sC = sAv(sl), sBv(sl), sCv(sl)
            for hh in range(2):
                E.op("g", lambda g, gk=gk, sl=sl, hh=hh: g.dma_start(
                    out=ab[:, sl * 4096 + hh * 2048:
                           sl * 4096 + (hh + 1) * 2048],
                    in_=xTb[:, gk * 4096 + hh * 2048:
                            gk * 4096 + (hh + 1) * 2048]),
                    writes=[xbk + "ab"[hh]], dma=True)
            for hh in range(2):
                E.op("s", lambda s, sl=sl, hh=hh: s.activation(
                    out=ab[:, 8192 + hh * 2048: 8192 + (hh + 1) * 2048],
                    in_=ab[:, sl * 4096 + hh * 2048:
                           sl * 4096 + (hh + 1) * 2048],
                    func=AF.Square),
                    reads=[xbk + "ab"[hh]], writes=["xsq" + "ab"[hh]])
            for hh in range(2):
                E.group("t", [
                    (lambda t, dc=dc, sl=sl: t.matmul(out=pbv[0],
                                                      lhsT=onesd[:],
                                                      rhs=xbg(sl, dc),
                                                      start=dc == 0,
                                                      stop=dc == 7))
                    for dc in range(4 * hh, 4 * hh + 4)],
                    reads=["onesd", xbk + "ab"[hh]], writes=["PB0"])
            for hh in range(2):
                E.group("t", [
                    (lambda t, dc=dc: t.matmul(out=pbv[1], lhsT=onesd[:],
                                               rhs=xsq(dc), start=dc == 0,
                                               stop=dc == 7))
                    for dc in range(4 * hh, 4 * hh + 4)],
                    reads=["onesd", "xsq" + "ab"[hh]], writes=["PB1"])

        def stats_tail(gk):
            sl = gk % 2
            sAk, sBk, sCk = f"sA{sl}", f"sB{sl}", f"sC{sl}"
            sA, sB, sC = sAv(sl), sBv(sl), sCv(sl)
            E.op("s", lambda s, sA=sA: s.copy(out=sA, in_=pbv[0]),
                 reads=["PB0"], writes=[sAk])
            E.op("v", lambda v, sA=sA, sB=sB: v.tensor_mul(out=sB, in0=sA,
                                                           in1=sA),
                 reads=[sAk], writes=[sBk])
            E.op("v", lambda v, sB=sB, sC=sC: v.tensor_sub(out=sC,
                                                           in0=pbv[1],
                                                           in1=sB),
                 reads=["PB1", sBk], writes=[sCk])
            E.op("s", lambda s, sB=sB, sC=sC: s.activation(out=sB, in_=sC,
                                                           func=AF.Ln,
                                                           bias=epsv[:]),
                 reads=[sCk, "epsv"], writes=[sBk])
            E.op("s", lambda s, sB=sB, sC=sC: s.activation(out=sC, in_=sB,
                                                           func=AF.Exp,
                                                           scale=-0.5),
                 reads=[sBk], writes=[sCk])

        def apply_part(gk):
            own = gk >= og0
            sl = gk % 2
            xbk = f"xbg{sl}"
            sAk, sCk = f"sA{sl}", f"sC{sl}"
            sA, sC = sAv(sl), sCv(sl)

            def dest_of(dc):
                if own:
                    return (xno(dc)[:, (gk - og0) * 512:(gk - og0 + 1) * 512],
                            f"xn{dc}")
                return xnt(dc), f"xt{dc}"

            for dc0 in range(0, 8, 2):
                for dc in (dc0, dc0 + 1):
                    sd, sdk = (sD0, "sD0") if dc % 2 == 0 else (sD1, "sD1")
                    E.op("v", lambda v, dc=dc, sl=sl, sA=sA, sd=sd:
                         v.tensor_sub(out=sd, in0=xbg(sl, dc), in1=sA),
                         reads=[xbk + ("a" if dc < 4 else "b"), sAk],
                         writes=[sdk])
                for dc in (dc0, dc0 + 1):
                    sd, sdk = (sD0, "sD0") if dc % 2 == 0 else (sD1, "sD1")
                    dest, dkey = dest_of(dc)
                    E.op("v", lambda v, dest=dest, sC=sC, sd=sd: v.tensor_mul(
                        out=dest, in0=sd, in1=sC),
                        reads=[sdk, sCk], writes=[dkey])

        def k_part(gk):
            own = gk >= og0
            src = (lambda dc, gk=gk: xno(dc)[:, (gk - og0) * 512:(gk - og0 + 1) * 512]) if own \
                else (lambda dc: xnt(dc))
            skeys = [f"xn{dc}" if own else f"xt{dc}" for dc in range(8)]
            for jc in range(8):
                bank = 2 + jc % 2
                E.group("t", [
                    (lambda t, dc=dc, jc=jc, bank=bank: t.matmul(
                        out=pbv[bank],
                        lhsT=wk_s[:, dc * 1024 + jc * 128: dc * 1024 + (jc + 1) * 128],
                        rhs=src(dc), start=dc == 0, stop=dc == 7))
                    for dc in range(8)],
                    reads=["wk"] + skeys, writes=[f"PB{bank}"])
                E.op("s", lambda s, jc=jc, bank=bank, gk=gk: s.activation(
                    out=kT(jc)[:, gk * 512:(gk + 1) * 512], in_=pbv[bank],
                    func=AF.Identity, bias=bias_s[:, 8 + jc: 9 + jc]),
                    reads=[f"PB{bank}", "biasp"], writes=[f"kT{jc}"])

        stats_part(0)
        stats_tail(0)
        for gk in range(kgn):
            if gk + 1 < kgn:
                stats_part(gk + 1)
            apply_part(gk)
            if gk + 1 < kgn:
                stats_tail(gk + 1)
            k_part(gk)
            emit_wload(6 if gk == 0 else 4)
        emit_wload(99)

        arena_src = (["xbg0a", "xbg0b", "xbg1a", "xbg1b", "xsqa", "xsqb",
                      "sD0", "sD1", "wst0", "wst1", "wst2", "wst3"]
                     + [f"sA{p}" for p in range(2)]
                     + [f"sB{p}" for p in range(2)]
                     + [f"sC{p}" for p in range(2)]
                     + [f"xt{dc}" for dc in range(8)])
        E.seal(["q0", "q1", "t1b", "g0", "g1", "xres0", "xres1",
                "out0", "out1", "c10", "c11", "rrow", "vs0", "vs1", "Rsb",
                "expo"],
               src_keys=arena_src)

        xnk = [f"xn{dc}" for dc in range(8)]

        def q_jit(ol, oslot):
            ocol = ol * 512
            qkey = f"q{oslot}"
            for jc in range(8):
                bank = 4 + jc % 2
                E.group("t", [
                    (lambda t, dc=dc, jc=jc, bank=bank, ocol=ocol: t.matmul(
                        out=pb[bank][:],
                        lhsT=wq_s[:, dc * 1024 + jc * 128: dc * 1024 + (jc + 1) * 128],
                        rhs=xno(dc)[:, ocol:ocol + 512],
                        start=dc == 0, stop=dc == 7))
                    for dc in range(8)],
                    reads=["wq"] + xnk, writes=[f"PB{bank}"])
                E.op("s", lambda s, jc=jc, bank=bank, oslot=oslot: s.activation(
                    out=qgrp(oslot, jc), in_=pb[bank][:],
                    func=AF.Identity, bias=bias_s[:, jc:jc + 1]),
                    reads=[f"PB{bank}", "biasp"], writes=[qkey])

        ccnt = [0]
        n_og = (hi - lo) // 4
        seq = list(reversed(range(n_og)))
        if len(seq) > 2:
            seq = [seq[0], seq[-1]] + seq[1:-1]
        q_jit(seq[0], 0)
        for si, ol in enumerate(seq):
            og = og0 + ol
            oslot = si % 2
            ocol = ol * 512
            qkey = f"q{oslot}"

            for i in range(4):
                t_ = og * 4 + i
                kext = 128 * (t_ + 1)
                nch = (kext + 1023) // 1024
                for c in range(nch):
                    w = 1024 if c < nch - 1 else kext - 1024 * (nch - 1)
                    par = ccnt[0] % 2
                    ccnt[0] += 1
                    big = pbBIG[par]
                    bkeys = [f"PB{2 * par}"] + \
                        ([f"PB{2 * par + 1}"] if w > 512 else [])
                    fns = []
                    for jc in range(8):
                        for hh in range(0, w, 512):
                            wh = min(512, w - hh)
                            fns.append(lambda t, jc=jc, c=c, i=i, hh=hh,
                                       wh=wh, big=big: t.matmul(
                                out=big[:, hh:hh + wh],
                                lhsT=qgrp(oslot, jc)[:, i * 128:(i + 1) * 128],
                                rhs=kT(jc)[:, c * 1024 + hh:
                                           c * 1024 + hh + wh],
                                start=jc == 0, stop=jc == 7))
                    E.group("t", fns,
                            reads=[qkey] + [f"kT{jc}" for jc in range(8)],
                            writes=bkeys)
                    if c == nch - 1:
                        m0 = w - 128
                        E.op("v", lambda v, big=big, m0=m0, w=w: v.tensor_mul(
                            out=big[:, m0:w], in0=big[:, m0:w],
                            in1=mask_s[:, 0:w - m0]),
                            reads=bkeys + ["maskp"], writes=bkeys)
                    ns, np_ = c % 2, (c + 1) % 2
                    E.op("s", lambda s, big=big, par=par, w=w: s.activation(
                        out=expo[:, 0:w], in_=big[:, 0:w], func=AF.Exp,
                        scale=INV_DK,
                        accum_out=csum[:, par:par + 1]),
                        reads=bkeys,
                        writes=["expo", f"cs{par}"])
                    if c == 0:
                        E.op("v", lambda v, par=par: v.tensor_copy(
                            out=sumb[:, 0:1], in_=csum[:, par:par + 1]),
                            reads=[f"cs{par}"], writes=["sum0"])
                    else:
                        E.op("v", lambda v, ns=ns, np_=np_, par=par:
                             v.tensor_add(
                                 out=sumb[:, ns:ns + 1],
                                 in0=sumb[:, np_:np_ + 1],
                                 in1=csum[:, par:par + 1]),
                             reads=[f"sum{np_}", f"cs{par}"],
                             writes=[f"sum{ns}"])
                last = (nch - 1) % 2
                zc = float(SEQ - kext)
                E.op("v", lambda v, zc=zc, last=last: v.tensor_scalar_add(
                    out=numb[:], in0=sumb[:, last:last + 1], scalar1=zc),
                    reads=[f"sum{last}"], writes=["num"])
                E.op("v", lambda v: v.reciprocal(out=zib[:], in_=numb[:]),
                     reads=["num"], writes=["zi"])
                E.op("v", lambda v, i=i: v.tensor_mul(
                    out=rgrp[:, i:i + 1], in0=numb[:], in1=zib[:]),
                    reads=["num", "zi"], writes=["rgrp"])
                if i == 2 and si + 1 < n_og:
                    q_jit(seq[si + 1], (si + 1) % 2)


            E.group("t", [
                (lambda t, dc=dc: t.matmul(
                    out=pb[5][0:32, :], lhsT=vr_s[:, dc * 32:(dc + 1) * 32],
                    rhs=xno(dc)[:, ocol:ocol + 512],
                    start=dc == 0, stop=dc == 7))
                for dc in range(8)], reads=["vr"] + xnk, writes=["PB5"])
            E.op("s", lambda s: s.copy(out=t1b[:], in_=pb[5][0:32, :]),
                 reads=["PB5"], writes=["t1b"])
            for oc in range(8):
                p = oc % 2
                wb = 6 if oc % 2 == 0 else 4
                vb = 7 if oc % 2 == 0 else 5
                E.op("g", lambda g, oc=oc, og=og: g.dma_start(
                    out=xres(oc % 2), in_=xTg[:, og * 4096 + oc * 512:
                                              og * 4096 + (oc + 1) * 512]),
                    writes=[f"xres{p}"], dma=True)
                fns = [
                    (lambda t, dc=dc, oc=oc, wb=wb: t.matmul(
                        out=pb[wb][:],
                        lhsT=wt_s[:, dc * 1024 + oc * 128: dc * 1024 + (oc + 1) * 128],
                        rhs=xno(dc)[:, ocol:ocol + 512],
                        start=dc == 0, stop=False))
                    for dc in range(8)]
                fns.append(lambda t, oc=oc, wb=wb: t.matmul(
                    out=pb[wb][:], lhsT=ct_s[:, oc * 128:(oc + 1) * 128],
                    rhs=t1b[:], start=False, stop=True))
                E.group("t", fns, reads=["wt", "ct", "t1b"] + xnk,
                        writes=[f"PB{wb}"])
                E.op("s", lambda s, oc=oc, wb=wb: s.activation(
                    out=grotb(oc % 2), in_=pb[wb][:], func=AF.Silu,
                    bias=bias_s[:, 24 + oc:25 + oc]),
                    reads=[f"PB{wb}", "biasp"], writes=[f"g{p}"])
                E.group("t", [
                    (lambda t, dc=dc, oc=oc, vb=vb: t.matmul(
                        out=pb[vb][:],
                        lhsT=wv_s[:, dc * 1024 + oc * 128: dc * 1024 + (oc + 1) * 128],
                        rhs=xno(dc)[:, ocol:ocol + 512],
                        start=dc == 0, stop=dc == 7))
                    for dc in range(8)], reads=["wv"] + xnk,
                    writes=[f"PB{vb}"])
                if oc == 0:
                    E.group("t", [
                        (lambda t, i=i: t.matmul(
                            out=pb[5][0:1, i * 128:(i + 1) * 128],
                            lhsT=rgrp[:, i:i + 1], rhs=ident_s[:],
                            start=True, stop=True))
                        for i in range(4)],
                        reads=["rgrp", "ident"], writes=["PB5"])
                    E.op("s", lambda s: s.copy(out=rrow[:], in_=pb[5][0:1, :]),
                         reads=["PB5"], writes=["rrow"])
                    E.op("t", lambda t: t.matmul(
                        out=pb[4][:], lhsT=ones1f[:], rhs=rrow[:],
                        start=True, stop=True),
                        reads=["ones1f", "rrow"], writes=["PB4"])
                    E.op("s", lambda s: s.copy(out=Rsb, in_=pb[4][:]),
                         reads=["PB4"], writes=["Rsb"])
                E.op("s", lambda s, oc=oc, vb=vb: s.activation(
                    out=vsb(oc % 2), in_=pb[vb][:], func=AF.Identity,
                    bias=bias_s[:, 16 + oc:17 + oc]),
                    reads=[f"PB{vb}", "biasp"], writes=[f"vs{p}"])
                ee = "g" if oc % 2 == 0 else "v"
                E.op(ee, lambda e, oc=oc: e.tensor_mul(
                    out=c1b(oc % 2), in0=vsb(oc % 2), in1=grotb(oc % 2)),
                    reads=[f"vs{p}", f"g{p}"], writes=[f"c1{p}"])
                E.op(ee, lambda e, oc=oc: e.tensor_mul(
                    out=c1b(oc % 2), in0=c1b(oc % 2), in1=Rsb),
                    reads=[f"c1{p}", "Rsb"], writes=[f"c1{p}"])
                E.op(ee, lambda e, oc=oc: e.tensor_add(
                    out=orot(oc % 2), in0=c1b(oc % 2), in1=xres(oc % 2)),
                    reads=[f"c1{p}", f"xres{p}"], writes=[f"out{p}"])
                E.op("y", lambda y, oc=oc, og=og: y.dma_start(
                    out=outT[oc * 128:(oc + 1) * 128,
                             og * 512:(og + 1) * 512],
                    in_=orot(oc % 2)),
                    reads=[f"out{p}"], dma=True)

        for i in range(Emitter.DMA_DEPTH):
            if E.cnt[f"dx{i}"]:
                nc.sync.wait_ge(sems[f"dx{i}"], E.cnt[f"dx{i}"])
            if E.cnt[f"dw{i}"]:
                nc.gpsimd.wait_ge(sems[f"dw{i}"], E.cnt[f"dw{i}"])

    if single_body is not None:
        emit_body(*((0, A_TILES) if single_body == 0 else (A_TILES, NT)))
    else:
        engines = [nc.tensor, nc.vector, nc.scalar, nc.sync, nc.gpsimd]
        idx = [e.partition_id() % 2 for e in engines]
        for b in nc.Switch(engines=engines, index=idx, n=2):
            if b == 0:
                emit_body(0, A_TILES)
            else:
                emit_body(A_TILES, NT)
    return nc


def _prep(X, Wq, bq, Wk, bk, Wv, bv, W, b, V_r, C):
    BF = ml_dtypes.bfloat16
    xTg = [np.ascontiguousarray(
        X[bb].reshape(8, 512, 8, 128).transpose(3, 0, 2, 1).reshape(128, 8 * SEQ))
        for bb in range(4)]
    xTb = [t.astype(BF) for t in xTg]
    biasp = np.empty((128, 32), dtype=np.float32)
    for wi, bb in enumerate((bq, bk, bv, b)):
        biasp[:, wi * 8:(wi + 1) * 8] = np.asarray(bb, np.float32).reshape(8, 128).T
    maskp = (np.arange(512)[None, :] <= np.arange(128)[:, None]).astype(BF)
    ident = np.eye(128, dtype=np.float32)
    shared = {
        "wq": np.ascontiguousarray(np.asarray(Wq, np.float32).astype(BF)),
        "wk": np.ascontiguousarray(np.asarray(Wk, np.float32).astype(BF)),
        "wv": np.ascontiguousarray(np.asarray(Wv, np.float32).astype(BF)),
        "wt": np.ascontiguousarray(np.asarray(W, np.float32).T.astype(BF)),
        "vr": np.ascontiguousarray(np.asarray(V_r, np.float32).astype(BF)),
        "ct": np.ascontiguousarray(np.asarray(C, np.float32).T.astype(BF)),
        "biasp": biasp, "maskp": maskp, "ident": ident,
    }
    return xTg, xTb, shared


def kernel(L0_attn_X, L0_attn_Wq, L0_attn_bq, L0_attn_Wk, L0_attn_bk,
           L0_attn_Wv, L0_attn_bv, L0_ffn_W, L0_ffn_b, L0_ffn_V_r, L0_ffn_C):
    X = np.asarray(L0_attn_X, np.float32)
    xTg, xTb, shared = _prep(X, L0_attn_Wq, L0_attn_bq, L0_attn_Wk,
                             L0_attn_bk, L0_attn_Wv, L0_attn_bv, L0_ffn_W,
                             L0_ffn_b, L0_ffn_V_r, L0_ffn_C)
    if "nc" not in _CACHE:
        _CACHE["nc"] = build_program()
    nc = _CACHE["nc"]
    in_maps = [dict(shared, xTg=xTg[c // 2], xTb=xTb[c // 2])
               for c in range(8)]
    res = run_bass_kernel_spmd(nc, in_maps, core_ids=list(range(8)))
    split = 128 * A_TILES
    out = np.empty((4, SEQ, D), dtype=np.float32)
    for bb in range(4):
        oa = np.asarray(res.results[2 * bb]["outT"])
        ob = np.asarray(res.results[2 * bb + 1]["outT"])
        out[bb, :split, :] = oa[:, :split].T
        out[bb, split:, :] = ob[:, split:].T
    return out


# version 25
# speedup vs baseline: 1.2220x; 1.0133x over previous
import math
import numpy as np
import ml_dtypes
import concourse.bass as bass
import concourse.mybir as mybir
from concourse.bass_utils import run_bass_kernel_spmd

F32 = mybir.dt.float32
BF16 = mybir.dt.bfloat16
AF = mybir.ActivationFunctionType
ALU = mybir.AluOpType
AX = mybir.AxisListType

A_TILES = 20
NT = 32
D = 1024
SEQ = 4096
INV_DK = 1.0 / 32.0


class Emitter:

    DMA_DEPTH = 6

    def __init__(self, nc, sems):
        self.nc = nc
        self.sems = sems
        self.cnt = {k: 0 for k in sems}
        self.lastw = {}
        self.readers = {}
        self.waited = {}
        self.dmasub = {"dx": 0, "dw": 0}
        self.eng = {
            "t": nc.tensor, "v": nc.vector, "s": nc.scalar,
            "y": nc.sync, "g": nc.gpsimd,
        }

    def _semname(self, e, dma):
        if e == "y":
            return "dx"
        if e == "g":
            return "dw" if dma else "gc"
        return {"t": "tt", "v": "vv", "s": "ss"}[e]

    def _wait(self, e, tok):
        sn, val = tok
        w = self.waited.setdefault(e, {})
        if w.get(sn, 0) >= val:
            return
        self.eng[e].wait_ge(self.sems[sn], val)
        w[sn] = val

    def _deps(self, e, reads, writes):
        toks = []
        for r in reads:
            t = self.lastw.get(r)
            if t:
                toks.append(t)
        for wkey in writes:
            t = self.lastw.get(wkey)
            if t:
                toks.append(t)
            toks.extend(self.readers.get(wkey, ()))
        best = {}
        for sn, val in toks:
            if best.get(sn, 0) < val:
                best[sn] = val
        for sn, val in best.items():
            self._wait(e, (sn, val))

    def _finish(self, e, inst, reads, writes, dma):
        sn = self._semname(e, dma)
        inc = 16 if dma else 1
        inst.then_inc(self.sems[sn], inc)
        self.cnt[sn] += inc
        tok = (sn, self.cnt[sn])
        for wkey in writes:
            self.lastw[wkey] = tok
            self.readers[wkey] = []
        for r in reads:
            self.readers.setdefault(r, []).append(tok)
        return tok

    def op(self, e, fn, reads=(), writes=(), dma=False):
        self._deps(e, reads, writes)
        if dma:
            q = self._semname(e, True)
            sn = f"{q}{self.dmasub[q]}"
            self.dmasub[q] = (self.dmasub[q] + 1) % self.DMA_DEPTH
            if self.cnt[sn] > 0:
                self._wait(e, (sn, self.cnt[sn]))
            inst = fn(self.eng[e])
            inst.then_inc(self.sems[sn], 16)
            self.cnt[sn] += 16
            tok = (sn, self.cnt[sn])
            for wkey in writes:
                self.lastw[wkey] = tok
                self.readers[wkey] = []
            for r in reads:
                self.readers.setdefault(r, []).append(tok)
            return tok
        inst = fn(self.eng[e])
        return self._finish(e, inst, reads, writes, dma)

    def group(self, e, fns, reads=(), writes=()):
        self._deps(e, reads, writes)
        inst = None
        for fn in fns:
            inst = fn(self.eng[e])
        return self._finish(e, inst, reads, writes, False)

    def seal(self, keys, src_keys=None):
        if src_keys is None:
            toks = [(sn, v) for sn, v in self.cnt.items() if v > 0]
        else:
            best = {}
            for k2 in src_keys:
                cand = list(self.readers.get(k2, ()))
                t = self.lastw.get(k2)
                if t:
                    cand.append(t)
                for sn, v in cand:
                    if best.get(sn, 0) < v:
                        best[sn] = v
            toks = list(best.items())
        for k in keys:
            self.readers[k] = list(toks)
            self.lastw.pop(k, None)


_CACHE = {}


def build_program(single_body=None):
    nc = bass.Bass("TRN2", target_bir_lowering=False)

    xTg = nc.dram_tensor("xTg", [128, 8 * SEQ], F32, kind="ExternalInput")
    xTb = nc.dram_tensor("xTb", [128, 8 * SEQ], BF16, kind="ExternalInput")
    wq = nc.dram_tensor("wq", [D, D], BF16, kind="ExternalInput")
    wk = nc.dram_tensor("wk", [D, D], BF16, kind="ExternalInput")
    wv = nc.dram_tensor("wv", [D, D], BF16, kind="ExternalInput")
    wt = nc.dram_tensor("wt", [D, D], BF16, kind="ExternalInput")
    vrd = nc.dram_tensor("vr", [D, 32], BF16, kind="ExternalInput")
    ctd = nc.dram_tensor("ct", [32, D], BF16, kind="ExternalInput")
    biasd = nc.dram_tensor("biasp", [128, 32], F32, kind="ExternalInput")
    maskd = nc.dram_tensor("maskp", [128, 512], BF16, kind="ExternalInput")
    identd = nc.dram_tensor("ident", [128, 128], F32, kind="ExternalInput")
    outT = nc.dram_tensor("outT", [D, SEQ], F32, kind="ExternalOutput")

    wq_s = nc.alloc_sbuf_tensor("wq_s", [128, 8192], BF16)
    wk_s = nc.alloc_sbuf_tensor("wk_s", [128, 8192], BF16)
    wv_s = nc.alloc_sbuf_tensor("wv_s", [128, 8192], BF16)
    wt_s = nc.alloc_sbuf_tensor("wt_s", [128, 8192], BF16)
    vr_s = nc.alloc_sbuf_tensor("vr_s", [128, 256], BF16)
    ct_s = nc.alloc_sbuf_tensor("ct_s", [32, 1024], BF16)
    bias_s = nc.alloc_sbuf_tensor("bias_s", [128, 32], F32)
    mask_s = nc.alloc_sbuf_tensor("mask_s", [128, 512], BF16)
    ident_s = nc.alloc_sbuf_tensor("ident_s", [128, 128], F32)
    onesd = nc.alloc_sbuf_tensor("onesd", [128, 128], BF16)
    ones1f = nc.alloc_sbuf_tensor("ones1f", [1, 128], F32)
    epsv = nc.alloc_sbuf_tensor("epsv", [128, 1], F32)
    rbig = nc.alloc_sbuf_tensor("rbig", [128, 45056], BF16)
    ab = nc.alloc_sbuf_tensor("ab", [128, 16384], BF16)
    af = nc.alloc_sbuf_tensor("af", [128, 4608], F32)
    cmax = nc.alloc_sbuf_tensor("cmax", [128, 1], F32)
    nmb = nc.alloc_sbuf_tensor("nmb", [128, 2], F32)
    sumb = nc.alloc_sbuf_tensor("sumb", [128, 2], F32)
    csum = nc.alloc_sbuf_tensor("csum", [128, 2], F32)
    dbuf = nc.alloc_sbuf_tensor("dbuf", [128, 1], F32)
    fbuf = nc.alloc_sbuf_tensor("fbuf", [128, 1], F32)
    ezb = nc.alloc_sbuf_tensor("ezb", [128, 1], F32)
    numb = nc.alloc_sbuf_tensor("numb", [128, 1], F32)
    zib = nc.alloc_sbuf_tensor("zib", [128, 1], F32)
    rgrp = nc.alloc_sbuf_tensor("rgrp", [128, 4], F32)

    pb01 = nc.alloc_psum_tensor("pb01", [128, 1024], F32)
    pb23 = nc.alloc_psum_tensor("pb23", [128, 1024], F32)
    pbv = {0: pb01[:, 0:512], 1: pb01[:, 512:1024],
           2: pb23[:, 0:512], 3: pb23[:, 512:1024]}
    pb = {i: nc.alloc_psum_tensor(f"pb{i}", [128, 512], F32)
          for i in range(4, 8)}
    pbBIG = [pb01, pb23]

    sems = {
        "tt": nc.alloc_semaphore("tt"), "vv": nc.alloc_semaphore("vv"),
        "ss": nc.alloc_semaphore("ss"), "gc": nc.alloc_semaphore("gc"),
    }
    for q in ("dx", "dw"):
        for i in range(Emitter.DMA_DEPTH):
            sems[f"{q}{i}"] = nc.alloc_semaphore(f"{q}{i}")

    def emit_body(lo, hi):
        E = Emitter(nc, sems)
        kgn = (hi + 3) // 4
        og0 = lo // 4
        kcols = kgn * 512
        ownlen = 128 * (hi - lo)

        def kT(jc):
            return rbig[:, jc * kcols:(jc + 1) * kcols]

        def xno(dc):
            base = 8 * kcols
            return rbig[:, base + dc * ownlen: base + (dc + 1) * ownlen]

        def xbg(slot, dc):
            return ab[:, slot * 4096 + dc * 512: slot * 4096 + (dc + 1) * 512]

        def xbg_all(slot):
            return ab[:, slot * 4096:(slot + 1) * 4096]

        def xsq(dc):
            return ab[:, 8192 + dc * 512: 8192 + (dc + 1) * 512]
        xsq_all = ab[:, 8192:12288]

        def xnt(dc):
            return ab[:, 12288 + dc * 512: 12288 + (dc + 1) * 512]

        def qgrp(slot, jc):
            return ab[:, slot * 4096 + jc * 512: slot * 4096 + (jc + 1) * 512]
        t1b = ab[0:32, 8192:8704]

        def grotb(p):
            return ab[:, 8704 + p * 512: 8704 + (p + 1) * 512]
        Rsb = ab[:, 10240:10752]
        expo = ab[:, 10752:11776]

        def sAv(sl):
            return af[:, sl * 1536: sl * 1536 + 512]

        def sBv(sl):
            return af[:, sl * 1536 + 512: sl * 1536 + 1024]

        def sCv(sl):
            return af[:, sl * 1536 + 1024: sl * 1536 + 1536]
        sD0 = af[:, 3072:3584]
        sD1 = af[:, 3584:4096]

        def wstage(p):
            return af[:, 2048 + p * 512: 2048 + (p + 1) * 512]

        def vsb(p):
            return af[:, p * 512:(p + 1) * 512]

        def xres(p):
            return af[:, 1024 + p * 512: 1024 + (p + 1) * 512]

        def orot(p):
            return af[:, 2048 + p * 512: 2048 + (p + 1) * 512]

        def c1b(p):
            return af[:, 3072 + p * 512: 3072 + (p + 1) * 512]
        rrow = af[0:1, 4096:4608]

        E.op("g", lambda g: g.dma_start(out=epsv[0:1, 0:1], in_=xTb[0:1, 0:1]),
             writes=["epsv"], dma=True)
        E.op("v", lambda v: v.memset(onesd[:], 1.0 / 1024.0), writes=["onesd"])
        E.op("v", lambda v: v.memset(ones1f[:], 1.0), writes=["ones1f"])
        E.op("v", lambda v: v.memset(epsv[:], 1e-5), writes=["epsv"])
        E.op("y", lambda y: y.dma_start(out=bias_s[:], in_=biasd[:]),
             writes=["biasp"], dma=True)
        E.op("y", lambda y: y.dma_start(out=ident_s[:], in_=identd[:]),
             writes=["ident"], dma=True)
        E.op("y", lambda y: y.dma_start(out=mask_s[:], in_=maskd[:]),
             writes=["maskp"], dma=True)
        for dc in range(8):
            E.op("y", lambda y, dc=dc: y.dma_start(
                out=wk_s[:, dc * 1024:(dc + 1) * 1024],
                in_=wk[dc * 128:(dc + 1) * 128, :]),
                writes=["wk"], dma=True)
        for dc in range(8):
            E.op("y", lambda y, dc=dc: y.dma_start(
                out=vr_s[:, dc * 32:(dc + 1) * 32],
                in_=vrd[dc * 128:(dc + 1) * 128, :]),
                writes=["vr"], dma=True)
        E.op("y", lambda y: y.dma_start(out=ct_s[:], in_=ctd[:]),
             writes=["ct"], dma=True)

        wload = []
        for nm, dram, sb in (("wq", wq, wq_s),
                             ("wv", wv, wv_s), ("wt", wt, wt_s)):
            for dc in range(8):
                wload.append((nm, dram, sb, dc))

        def emit_wload(n):
            while wload and n > 0:
                nm, dram, sb, dc = wload.pop(0)
                E.op("g", lambda g, dram=dram, sb=sb, dc=dc: g.dma_start(
                    out=sb[:, dc * 1024:(dc + 1) * 1024],
                    in_=dram[dc * 128:(dc + 1) * 128, :]),
                    writes=[nm], dma=True)
                n -= 1

        def stats_part(gk):
            sl = gk % 2
            xbk = f"xbg{sl}"
            sAk, sBk, sCk = f"sA{sl}", f"sB{sl}", f"sC{sl}"
            sA, sB, sC = sAv(sl), sBv(sl), sCv(sl)
            for hh in range(2):
                E.op("g", lambda g, gk=gk, sl=sl, hh=hh: g.dma_start(
                    out=ab[:, sl * 4096 + hh * 2048:
                           sl * 4096 + (hh + 1) * 2048],
                    in_=xTb[:, gk * 4096 + hh * 2048:
                            gk * 4096 + (hh + 1) * 2048]),
                    writes=[xbk + "ab"[hh]], dma=True)
            for hh in range(2):
                E.op("s", lambda s, sl=sl, hh=hh: s.activation(
                    out=ab[:, 8192 + hh * 2048: 8192 + (hh + 1) * 2048],
                    in_=ab[:, sl * 4096 + hh * 2048:
                           sl * 4096 + (hh + 1) * 2048],
                    func=AF.Square),
                    reads=[xbk + "ab"[hh]], writes=["xsq" + "ab"[hh]])
            for hh in range(2):
                E.group("t", [
                    (lambda t, dc=dc, sl=sl: t.matmul(out=pbv[0],
                                                      lhsT=onesd[:],
                                                      rhs=xbg(sl, dc),
                                                      start=dc == 0,
                                                      stop=dc == 7))
                    for dc in range(4 * hh, 4 * hh + 4)],
                    reads=["onesd", xbk + "ab"[hh]], writes=["PB0"])
            for hh in range(2):
                E.group("t", [
                    (lambda t, dc=dc: t.matmul(out=pbv[1], lhsT=onesd[:],
                                               rhs=xsq(dc), start=dc == 0,
                                               stop=dc == 7))
                    for dc in range(4 * hh, 4 * hh + 4)],
                    reads=["onesd", "xsq" + "ab"[hh]], writes=["PB1"])

        def stats_tail(gk):
            sl = gk % 2
            sAk, sBk, sCk = f"sA{sl}", f"sB{sl}", f"sC{sl}"
            sA, sB, sC = sAv(sl), sBv(sl), sCv(sl)
            E.op("s", lambda s, sA=sA: s.copy(out=sA, in_=pbv[0]),
                 reads=["PB0"], writes=[sAk])
            E.op("v", lambda v, sA=sA, sB=sB: v.tensor_mul(out=sB, in0=sA,
                                                           in1=sA),
                 reads=[sAk], writes=[sBk])
            E.op("v", lambda v, sB=sB, sC=sC: v.tensor_sub(out=sC,
                                                           in0=pbv[1],
                                                           in1=sB),
                 reads=["PB1", sBk], writes=[sCk])
            E.op("s", lambda s, sB=sB, sC=sC: s.activation(out=sB, in_=sC,
                                                           func=AF.Ln,
                                                           bias=epsv[:]),
                 reads=[sCk, "epsv"], writes=[sBk])
            E.op("s", lambda s, sB=sB, sC=sC: s.activation(out=sC, in_=sB,
                                                           func=AF.Exp,
                                                           scale=-0.5),
                 reads=[sBk], writes=[sCk])

        def apply_part(gk):
            own = gk >= og0
            sl = gk % 2
            xbk = f"xbg{sl}"
            sAk, sCk = f"sA{sl}", f"sC{sl}"
            sA, sC = sAv(sl), sCv(sl)

            def dest_of(dc):
                if own:
                    return (xno(dc)[:, (gk - og0) * 512:(gk - og0 + 1) * 512],
                            f"xn{dc}")
                return xnt(dc), f"xt{dc}"

            for dc0 in range(0, 8, 2):
                for dc in (dc0, dc0 + 1):
                    sd, sdk = (sD0, "sD0") if dc % 2 == 0 else (sD1, "sD1")
                    E.op("v", lambda v, dc=dc, sl=sl, sA=sA, sd=sd:
                         v.tensor_sub(out=sd, in0=xbg(sl, dc), in1=sA),
                         reads=[xbk + ("a" if dc < 4 else "b"), sAk],
                         writes=[sdk])
                for dc in (dc0, dc0 + 1):
                    sd, sdk = (sD0, "sD0") if dc % 2 == 0 else (sD1, "sD1")
                    dest, dkey = dest_of(dc)
                    E.op("v", lambda v, dest=dest, sC=sC, sd=sd: v.tensor_mul(
                        out=dest, in0=sd, in1=sC),
                        reads=[sdk, sCk], writes=[dkey])

        def k_part(gk):
            own = gk >= og0
            src = (lambda dc, gk=gk: xno(dc)[:, (gk - og0) * 512:(gk - og0 + 1) * 512]) if own \
                else (lambda dc: xnt(dc))
            skeys = [f"xn{dc}" if own else f"xt{dc}" for dc in range(8)]
            for jc in range(8):
                bank = 2 + jc % 2
                E.group("t", [
                    (lambda t, dc=dc, jc=jc, bank=bank: t.matmul(
                        out=pbv[bank],
                        lhsT=wk_s[:, dc * 1024 + jc * 128: dc * 1024 + (jc + 1) * 128],
                        rhs=src(dc), start=dc == 0, stop=dc == 7))
                    for dc in range(8)],
                    reads=["wk"] + skeys, writes=[f"PB{bank}"])
                E.op("s", lambda s, jc=jc, bank=bank, gk=gk: s.activation(
                    out=kT(jc)[:, gk * 512:(gk + 1) * 512], in_=pbv[bank],
                    func=AF.Identity, bias=bias_s[:, 8 + jc: 9 + jc]),
                    reads=[f"PB{bank}", "biasp"], writes=[f"kT{jc}"])

        stats_part(0)
        stats_tail(0)
        for gk in range(kgn):
            if gk + 1 < kgn:
                stats_part(gk + 1)
            apply_part(gk)
            if gk + 1 < kgn:
                stats_tail(gk + 1)
            k_part(gk)
            emit_wload(6 if gk == 0 else 4)
        emit_wload(99)

        arena_src = (["xbg0a", "xbg0b", "xbg1a", "xbg1b", "xsqa", "xsqb",
                      "sD0", "sD1", "wst0", "wst1", "wst2", "wst3"]
                     + [f"sA{p}" for p in range(2)]
                     + [f"sB{p}" for p in range(2)]
                     + [f"sC{p}" for p in range(2)]
                     + [f"xt{dc}" for dc in range(8)])
        E.seal(["q0", "q1", "t1b", "g0", "g1", "xres0", "xres1",
                "out0", "out1", "c10", "c11", "rrow", "vs0", "vs1", "Rsb",
                "expo"],
               src_keys=arena_src)

        xnk = [f"xn{dc}" for dc in range(8)]

        def q_jit(ol, oslot):
            ocol = ol * 512
            qkey = f"q{oslot}"
            for jc in range(8):
                bank = 4 + jc % 2
                E.group("t", [
                    (lambda t, dc=dc, jc=jc, bank=bank, ocol=ocol: t.matmul(
                        out=pb[bank][:],
                        lhsT=wq_s[:, dc * 1024 + jc * 128: dc * 1024 + (jc + 1) * 128],
                        rhs=xno(dc)[:, ocol:ocol + 512],
                        start=dc == 0, stop=dc == 7))
                    for dc in range(8)],
                    reads=["wq"] + xnk, writes=[f"PB{bank}"])
                E.op("s", lambda s, jc=jc, bank=bank, oslot=oslot: s.activation(
                    out=qgrp(oslot, jc), in_=pb[bank][:],
                    func=AF.Identity, bias=bias_s[:, jc:jc + 1]),
                    reads=[f"PB{bank}", "biasp"], writes=[qkey])

        ccnt = [0]
        n_og = (hi - lo) // 4
        seq = list(reversed(range(n_og)))
        if len(seq) > 2:
            seq = [seq[0], seq[-1]] + seq[1:-1]
        q_jit(seq[0], 0)
        for si, ol in enumerate(seq):
            og = og0 + ol
            oslot = si % 2
            ocol = ol * 512
            qkey = f"q{oslot}"

            for i in range(4):
                t_ = og * 4 + i
                kext = 128 * (t_ + 1)
                nch = (kext + 1023) // 1024
                for c in range(nch):
                    w = 1024 if c < nch - 1 else kext - 1024 * (nch - 1)
                    par = ccnt[0] % 2
                    ccnt[0] += 1
                    big = pbBIG[par]
                    bkeys = [f"PB{2 * par}"] + \
                        ([f"PB{2 * par + 1}"] if w > 512 else [])
                    fns = []
                    for jc in range(8):
                        for hh in range(0, w, 512):
                            wh = min(512, w - hh)
                            fns.append(lambda t, jc=jc, c=c, i=i, hh=hh,
                                       wh=wh, big=big: t.matmul(
                                out=big[:, hh:hh + wh],
                                lhsT=qgrp(oslot, jc)[:, i * 128:(i + 1) * 128],
                                rhs=kT(jc)[:, c * 1024 + hh:
                                           c * 1024 + hh + wh],
                                start=jc == 0, stop=jc == 7))
                    E.group("t", fns,
                            reads=[qkey] + [f"kT{jc}" for jc in range(8)],
                            writes=bkeys)
                    if c == nch - 1:
                        m0 = w - 128
                        E.op("v", lambda v, big=big, m0=m0, w=w: v.tensor_mul(
                            out=big[:, m0:w], in0=big[:, m0:w],
                            in1=mask_s[:, 0:w - m0]),
                            reads=bkeys + ["maskp"], writes=bkeys)
                    ns, np_ = c % 2, (c + 1) % 2
                    E.op("s", lambda s, big=big, par=par, w=w: s.activation(
                        out=expo[:, 0:w], in_=big[:, 0:w], func=AF.Exp,
                        scale=INV_DK,
                        accum_out=csum[:, par:par + 1]),
                        reads=bkeys,
                        writes=["expo", f"cs{par}"])
                    if c == 0:
                        E.op("v", lambda v, par=par: v.tensor_copy(
                            out=sumb[:, 0:1], in_=csum[:, par:par + 1]),
                            reads=[f"cs{par}"], writes=["sum0"])
                    else:
                        E.op("v", lambda v, ns=ns, np_=np_, par=par:
                             v.tensor_add(
                                 out=sumb[:, ns:ns + 1],
                                 in0=sumb[:, np_:np_ + 1],
                                 in1=csum[:, par:par + 1]),
                             reads=[f"sum{np_}", f"cs{par}"],
                             writes=[f"sum{ns}"])
                last = (nch - 1) % 2
                zc = float(SEQ - kext)
                E.op("v", lambda v, zc=zc, last=last: v.tensor_scalar_add(
                    out=numb[:], in0=sumb[:, last:last + 1], scalar1=zc),
                    reads=[f"sum{last}"], writes=["num"])
                E.op("v", lambda v: v.reciprocal(out=zib[:], in_=numb[:]),
                     reads=["num"], writes=["zi"])
                E.op("v", lambda v, i=i: v.tensor_mul(
                    out=rgrp[:, i:i + 1], in0=numb[:], in1=zib[:]),
                    reads=["num", "zi"], writes=["rgrp"])
                if i == 2 and si + 1 < n_og:
                    q_jit(seq[si + 1], (si + 1) % 2)


            E.group("t", [
                (lambda t, dc=dc: t.matmul(
                    out=pb[5][0:32, :], lhsT=vr_s[:, dc * 32:(dc + 1) * 32],
                    rhs=xno(dc)[:, ocol:ocol + 512],
                    start=dc == 0, stop=dc == 7))
                for dc in range(8)], reads=["vr"] + xnk, writes=["PB5"])
            E.op("s", lambda s: s.copy(out=t1b[:], in_=pb[5][0:32, :]),
                 reads=["PB5"], writes=["t1b"])
            for oc in range(8):
                p = oc % 2
                wb = 6 if oc % 2 == 0 else 4
                vb = 7 if oc % 2 == 0 else 5
                E.op("y", lambda y, oc=oc, og=og: y.dma_start(
                    out=xres(oc % 2), in_=xTg[:, og * 4096 + oc * 512:
                                              og * 4096 + (oc + 1) * 512]),
                    writes=[f"xres{p}"], dma=True)
                fns = [
                    (lambda t, dc=dc, oc=oc, wb=wb: t.matmul(
                        out=pb[wb][:],
                        lhsT=wt_s[:, dc * 1024 + oc * 128: dc * 1024 + (oc + 1) * 128],
                        rhs=xno(dc)[:, ocol:ocol + 512],
                        start=dc == 0, stop=False))
                    for dc in range(8)]
                fns.append(lambda t, oc=oc, wb=wb: t.matmul(
                    out=pb[wb][:], lhsT=ct_s[:, oc * 128:(oc + 1) * 128],
                    rhs=t1b[:], start=False, stop=True))
                E.group("t", fns, reads=["wt", "ct", "t1b"] + xnk,
                        writes=[f"PB{wb}"])
                E.op("s", lambda s, oc=oc, wb=wb: s.activation(
                    out=grotb(oc % 2), in_=pb[wb][:], func=AF.Silu,
                    bias=bias_s[:, 24 + oc:25 + oc]),
                    reads=[f"PB{wb}", "biasp"], writes=[f"g{p}"])
                E.group("t", [
                    (lambda t, dc=dc, oc=oc, vb=vb: t.matmul(
                        out=pb[vb][:],
                        lhsT=wv_s[:, dc * 1024 + oc * 128: dc * 1024 + (oc + 1) * 128],
                        rhs=xno(dc)[:, ocol:ocol + 512],
                        start=dc == 0, stop=dc == 7))
                    for dc in range(8)], reads=["wv"] + xnk,
                    writes=[f"PB{vb}"])
                if oc == 0:
                    E.group("t", [
                        (lambda t, i=i: t.matmul(
                            out=pb[5][0:1, i * 128:(i + 1) * 128],
                            lhsT=rgrp[:, i:i + 1], rhs=ident_s[:],
                            start=True, stop=True))
                        for i in range(4)],
                        reads=["rgrp", "ident"], writes=["PB5"])
                    E.op("s", lambda s: s.copy(out=rrow[:], in_=pb[5][0:1, :]),
                         reads=["PB5"], writes=["rrow"])
                    E.op("t", lambda t: t.matmul(
                        out=pb[4][:], lhsT=ones1f[:], rhs=rrow[:],
                        start=True, stop=True),
                        reads=["ones1f", "rrow"], writes=["PB4"])
                    E.op("s", lambda s: s.copy(out=Rsb, in_=pb[4][:]),
                         reads=["PB4"], writes=["Rsb"])
                E.op("s", lambda s, oc=oc, vb=vb: s.activation(
                    out=vsb(oc % 2), in_=pb[vb][:], func=AF.Identity,
                    bias=bias_s[:, 16 + oc:17 + oc]),
                    reads=[f"PB{vb}", "biasp"], writes=[f"vs{p}"])
                ee = "g" if oc % 2 == 0 else "v"
                E.op(ee, lambda e, oc=oc: e.tensor_mul(
                    out=c1b(oc % 2), in0=vsb(oc % 2), in1=grotb(oc % 2)),
                    reads=[f"vs{p}", f"g{p}"], writes=[f"c1{p}"])
                E.op(ee, lambda e, oc=oc: e.tensor_mul(
                    out=c1b(oc % 2), in0=c1b(oc % 2), in1=Rsb),
                    reads=[f"c1{p}", "Rsb"], writes=[f"c1{p}"])
                E.op(ee, lambda e, oc=oc: e.tensor_add(
                    out=orot(oc % 2), in0=c1b(oc % 2), in1=xres(oc % 2)),
                    reads=[f"c1{p}", f"xres{p}"], writes=[f"out{p}"])
                E.op("g", lambda g, oc=oc, og=og: g.dma_start(
                    out=outT[oc * 128:(oc + 1) * 128,
                             og * 512:(og + 1) * 512],
                    in_=orot(oc % 2)),
                    reads=[f"out{p}"], dma=True)

        for i in range(Emitter.DMA_DEPTH):
            if E.cnt[f"dx{i}"]:
                nc.sync.wait_ge(sems[f"dx{i}"], E.cnt[f"dx{i}"])
            if E.cnt[f"dw{i}"]:
                nc.gpsimd.wait_ge(sems[f"dw{i}"], E.cnt[f"dw{i}"])

    if single_body is not None:
        emit_body(*((0, A_TILES) if single_body == 0 else (A_TILES, NT)))
    else:
        engines = [nc.tensor, nc.vector, nc.scalar, nc.sync, nc.gpsimd]
        idx = [e.partition_id() % 2 for e in engines]
        for b in nc.Switch(engines=engines, index=idx, n=2):
            if b == 0:
                emit_body(0, A_TILES)
            else:
                emit_body(A_TILES, NT)
    return nc


def _prep(X, Wq, bq, Wk, bk, Wv, bv, W, b, V_r, C):
    BF = ml_dtypes.bfloat16
    xTg = [np.ascontiguousarray(
        X[bb].reshape(8, 512, 8, 128).transpose(3, 0, 2, 1).reshape(128, 8 * SEQ))
        for bb in range(4)]
    xTb = [t.astype(BF) for t in xTg]
    biasp = np.empty((128, 32), dtype=np.float32)
    for wi, bb in enumerate((bq, bk, bv, b)):
        biasp[:, wi * 8:(wi + 1) * 8] = np.asarray(bb, np.float32).reshape(8, 128).T
    maskp = (np.arange(512)[None, :] <= np.arange(128)[:, None]).astype(BF)
    ident = np.eye(128, dtype=np.float32)
    shared = {
        "wq": np.ascontiguousarray(np.asarray(Wq, np.float32).astype(BF)),
        "wk": np.ascontiguousarray(np.asarray(Wk, np.float32).astype(BF)),
        "wv": np.ascontiguousarray(np.asarray(Wv, np.float32).astype(BF)),
        "wt": np.ascontiguousarray(np.asarray(W, np.float32).T.astype(BF)),
        "vr": np.ascontiguousarray(np.asarray(V_r, np.float32).astype(BF)),
        "ct": np.ascontiguousarray(np.asarray(C, np.float32).T.astype(BF)),
        "biasp": biasp, "maskp": maskp, "ident": ident,
    }
    return xTg, xTb, shared


def kernel(L0_attn_X, L0_attn_Wq, L0_attn_bq, L0_attn_Wk, L0_attn_bk,
           L0_attn_Wv, L0_attn_bv, L0_ffn_W, L0_ffn_b, L0_ffn_V_r, L0_ffn_C):
    X = np.asarray(L0_attn_X, np.float32)
    xTg, xTb, shared = _prep(X, L0_attn_Wq, L0_attn_bq, L0_attn_Wk,
                             L0_attn_bk, L0_attn_Wv, L0_attn_bv, L0_ffn_W,
                             L0_ffn_b, L0_ffn_V_r, L0_ffn_C)
    if "nc" not in _CACHE:
        _CACHE["nc"] = build_program()
    nc = _CACHE["nc"]
    in_maps = [dict(shared, xTg=xTg[c // 2], xTb=xTb[c // 2])
               for c in range(8)]
    res = run_bass_kernel_spmd(nc, in_maps, core_ids=list(range(8)))
    split = 128 * A_TILES
    out = np.empty((4, SEQ, D), dtype=np.float32)
    for bb in range(4):
        oa = np.asarray(res.results[2 * bb]["outT"])
        ob = np.asarray(res.results[2 * bb + 1]["outT"])
        out[bb, :split, :] = oa[:, :split].T
        out[bb, split:, :] = ob[:, split:].T
    return out


# version 27
# speedup vs baseline: 1.2520x; 1.0245x over previous
import math
import numpy as np
import ml_dtypes
import concourse.bass as bass
import concourse.mybir as mybir
from concourse.bass_utils import run_bass_kernel_spmd

F32 = mybir.dt.float32
BF16 = mybir.dt.bfloat16
AF = mybir.ActivationFunctionType
ALU = mybir.AluOpType
AX = mybir.AxisListType

A_TILES = 20
NT = 32
D = 1024
SEQ = 4096
INV_DK = 1.0 / 32.0


class Emitter:

    DMA_DEPTH = 6

    def __init__(self, nc, sems):
        self.nc = nc
        self.sems = sems
        self.cnt = {k: 0 for k in sems}
        self.lastw = {}
        self.readers = {}
        self.waited = {}
        self.dmasub = {"dx": 0, "dw": 0}
        self.eng = {
            "t": nc.tensor, "v": nc.vector, "s": nc.scalar,
            "y": nc.sync, "g": nc.gpsimd,
        }

    def _semname(self, e, dma):
        if e == "y":
            return "dx"
        if e == "g":
            return "dw" if dma else "gc"
        return {"t": "tt", "v": "vv", "s": "ss"}[e]

    def _wait(self, e, tok):
        sn, val = tok
        if e == "t" and sn == "tt":
            return
        w = self.waited.setdefault(e, {})
        if w.get(sn, 0) >= val:
            return
        self.eng[e].wait_ge(self.sems[sn], val)
        w[sn] = val

    def _deps(self, e, reads, writes):
        toks = []
        for r in reads:
            t = self.lastw.get(r)
            if t:
                toks.append(t)
        for wkey in writes:
            t = self.lastw.get(wkey)
            if t:
                toks.append(t)
            toks.extend(self.readers.get(wkey, ()))
        best = {}
        for sn, val in toks:
            if best.get(sn, 0) < val:
                best[sn] = val
        for sn, val in best.items():
            self._wait(e, (sn, val))

    def _finish(self, e, inst, reads, writes, dma):
        sn = self._semname(e, dma)
        inc = 16 if dma else 1
        inst.then_inc(self.sems[sn], inc)
        self.cnt[sn] += inc
        tok = (sn, self.cnt[sn])
        for wkey in writes:
            self.lastw[wkey] = tok
            self.readers[wkey] = []
        for r in reads:
            self.readers.setdefault(r, []).append(tok)
        return tok

    def op(self, e, fn, reads=(), writes=(), dma=False):
        self._deps(e, reads, writes)
        if dma:
            q = self._semname(e, True)
            sn = f"{q}{self.dmasub[q]}"
            self.dmasub[q] = (self.dmasub[q] + 1) % self.DMA_DEPTH
            if self.cnt[sn] > 0:
                self._wait(e, (sn, self.cnt[sn]))
            inst = fn(self.eng[e])
            inst.then_inc(self.sems[sn], 16)
            self.cnt[sn] += 16
            tok = (sn, self.cnt[sn])
            for wkey in writes:
                self.lastw[wkey] = tok
                self.readers[wkey] = []
            for r in reads:
                self.readers.setdefault(r, []).append(tok)
            return tok
        inst = fn(self.eng[e])
        return self._finish(e, inst, reads, writes, dma)

    def group(self, e, fns, reads=(), writes=()):
        self._deps(e, reads, writes)
        inst = None
        for fn in fns:
            inst = fn(self.eng[e])
        return self._finish(e, inst, reads, writes, False)

    def seal(self, keys, src_keys=None):
        if src_keys is None:
            toks = [(sn, v) for sn, v in self.cnt.items() if v > 0]
        else:
            best = {}
            for k2 in src_keys:
                cand = list(self.readers.get(k2, ()))
                t = self.lastw.get(k2)
                if t:
                    cand.append(t)
                for sn, v in cand:
                    if best.get(sn, 0) < v:
                        best[sn] = v
            toks = list(best.items())
        for k in keys:
            self.readers[k] = list(toks)
            self.lastw.pop(k, None)


_CACHE = {}


def build_program(single_body=None):
    nc = bass.Bass("TRN2", target_bir_lowering=False)

    xTg = nc.dram_tensor("xTg", [128, 8 * SEQ], F32, kind="ExternalInput")
    xTb = nc.dram_tensor("xTb", [128, 8 * SEQ], BF16, kind="ExternalInput")
    wq = nc.dram_tensor("wq", [D, D], BF16, kind="ExternalInput")
    wk = nc.dram_tensor("wk", [D, D], BF16, kind="ExternalInput")
    wv = nc.dram_tensor("wv", [D, D], BF16, kind="ExternalInput")
    wt = nc.dram_tensor("wt", [D, D], BF16, kind="ExternalInput")
    vrd = nc.dram_tensor("vr", [D, 32], BF16, kind="ExternalInput")
    ctd = nc.dram_tensor("ct", [32, D], BF16, kind="ExternalInput")
    biasd = nc.dram_tensor("biasp", [128, 32], F32, kind="ExternalInput")
    maskd = nc.dram_tensor("maskp", [128, 512], BF16, kind="ExternalInput")
    identd = nc.dram_tensor("ident", [128, 128], F32, kind="ExternalInput")
    outT = nc.dram_tensor("outT", [D, SEQ], F32, kind="ExternalOutput")

    wq_s = nc.alloc_sbuf_tensor("wq_s", [128, 8192], BF16)
    wk_s = nc.alloc_sbuf_tensor("wk_s", [128, 8192], BF16)
    wv_s = nc.alloc_sbuf_tensor("wv_s", [128, 8192], BF16)
    wt_s = nc.alloc_sbuf_tensor("wt_s", [128, 8192], BF16)
    vr_s = nc.alloc_sbuf_tensor("vr_s", [128, 256], BF16)
    ct_s = nc.alloc_sbuf_tensor("ct_s", [32, 1024], BF16)
    bias_s = nc.alloc_sbuf_tensor("bias_s", [128, 32], F32)
    mask_s = nc.alloc_sbuf_tensor("mask_s", [128, 512], BF16)
    ident_s = nc.alloc_sbuf_tensor("ident_s", [128, 128], F32)
    onesd = nc.alloc_sbuf_tensor("onesd", [128, 128], BF16)
    ones1f = nc.alloc_sbuf_tensor("ones1f", [1, 128], F32)
    epsv = nc.alloc_sbuf_tensor("epsv", [128, 1], F32)
    rbig = nc.alloc_sbuf_tensor("rbig", [128, 45056], BF16)
    ab = nc.alloc_sbuf_tensor("ab", [128, 16384], BF16)
    af = nc.alloc_sbuf_tensor("af", [128, 4608], F32)
    cmax = nc.alloc_sbuf_tensor("cmax", [128, 1], F32)
    nmb = nc.alloc_sbuf_tensor("nmb", [128, 2], F32)
    sumb = nc.alloc_sbuf_tensor("sumb", [128, 2], F32)
    csum = nc.alloc_sbuf_tensor("csum", [128, 2], F32)
    dbuf = nc.alloc_sbuf_tensor("dbuf", [128, 1], F32)
    fbuf = nc.alloc_sbuf_tensor("fbuf", [128, 1], F32)
    ezb = nc.alloc_sbuf_tensor("ezb", [128, 1], F32)
    numb = nc.alloc_sbuf_tensor("numb", [128, 1], F32)
    zib = nc.alloc_sbuf_tensor("zib", [128, 1], F32)
    rgrp = nc.alloc_sbuf_tensor("rgrp", [128, 4], F32)

    pb01 = nc.alloc_psum_tensor("pb01", [128, 1024], F32)
    pb23 = nc.alloc_psum_tensor("pb23", [128, 1024], F32)
    pbv = {0: pb01[:, 0:512], 1: pb01[:, 512:1024],
           2: pb23[:, 0:512], 3: pb23[:, 512:1024]}
    pb = {i: nc.alloc_psum_tensor(f"pb{i}", [128, 512], F32)
          for i in range(4, 8)}
    pbBIG = [pb01, pb23]

    sems = {
        "tt": nc.alloc_semaphore("tt"), "vv": nc.alloc_semaphore("vv"),
        "ss": nc.alloc_semaphore("ss"), "gc": nc.alloc_semaphore("gc"),
    }
    for q in ("dx", "dw"):
        for i in range(Emitter.DMA_DEPTH):
            sems[f"{q}{i}"] = nc.alloc_semaphore(f"{q}{i}")

    def emit_body(lo, hi):
        E = Emitter(nc, sems)
        kgn = (hi + 3) // 4
        og0 = lo // 4
        kcols = kgn * 512
        ownlen = 128 * (hi - lo)

        def kT(jc):
            return rbig[:, jc * kcols:(jc + 1) * kcols]

        def xno(dc):
            base = 8 * kcols
            return rbig[:, base + dc * ownlen: base + (dc + 1) * ownlen]

        def xbg(slot, dc):
            return ab[:, slot * 4096 + dc * 512: slot * 4096 + (dc + 1) * 512]

        def xbg_all(slot):
            return ab[:, slot * 4096:(slot + 1) * 4096]

        def xsq(dc):
            return ab[:, 8192 + dc * 512: 8192 + (dc + 1) * 512]
        xsq_all = ab[:, 8192:12288]

        def xnt(dc):
            return ab[:, 12288 + dc * 512: 12288 + (dc + 1) * 512]

        def qgrp(slot, jc):
            return ab[:, slot * 4096 + jc * 512: slot * 4096 + (jc + 1) * 512]
        t1b = ab[0:32, 8192:8704]

        def grotb(p):
            return ab[:, 8704 + p * 512: 8704 + (p + 1) * 512]
        Rsb = ab[:, 10240:10752]
        expo = ab[:, 10752:11776]

        def sAv(sl):
            return af[:, sl * 1536: sl * 1536 + 512]

        def sBv(sl):
            return af[:, sl * 1536 + 512: sl * 1536 + 1024]

        def sCv(sl):
            return af[:, sl * 1536 + 1024: sl * 1536 + 1536]
        sD0 = af[:, 3072:3584]
        sD1 = af[:, 3584:4096]

        def wstage(p):
            return af[:, 2048 + p * 512: 2048 + (p + 1) * 512]

        def vsb(p):
            return af[:, p * 512:(p + 1) * 512]

        def xres(p):
            return af[:, 1024 + p * 512: 1024 + (p + 1) * 512]

        def orot(p):
            return af[:, 2048 + p * 512: 2048 + (p + 1) * 512]

        def c1b(p):
            return af[:, 3072 + p * 512: 3072 + (p + 1) * 512]
        rrow = af[0:1, 4096:4608]

        E.op("g", lambda g: g.dma_start(out=epsv[0:1, 0:1], in_=xTb[0:1, 0:1]),
             writes=["epsv"], dma=True)
        E.op("v", lambda v: v.memset(onesd[:], 1.0 / 1024.0), writes=["onesd"])
        E.op("v", lambda v: v.memset(ones1f[:], 1.0), writes=["ones1f"])
        E.op("v", lambda v: v.memset(epsv[:], 1e-5), writes=["epsv"])
        E.op("y", lambda y: y.dma_start(out=bias_s[:], in_=biasd[:]),
             writes=["biasp"], dma=True)
        E.op("y", lambda y: y.dma_start(out=ident_s[:], in_=identd[:]),
             writes=["ident"], dma=True)
        E.op("y", lambda y: y.dma_start(out=mask_s[:], in_=maskd[:]),
             writes=["maskp"], dma=True)
        for dc in range(8):
            E.op("y", lambda y, dc=dc: y.dma_start(
                out=wk_s[:, dc * 1024:(dc + 1) * 1024],
                in_=wk[dc * 128:(dc + 1) * 128, :]),
                writes=["wk"], dma=True)
        for dc in range(8):
            E.op("y", lambda y, dc=dc: y.dma_start(
                out=vr_s[:, dc * 32:(dc + 1) * 32],
                in_=vrd[dc * 128:(dc + 1) * 128, :]),
                writes=["vr"], dma=True)
        E.op("y", lambda y: y.dma_start(out=ct_s[:], in_=ctd[:]),
             writes=["ct"], dma=True)

        wload = []
        for nm, dram, sb in (("wq", wq, wq_s),
                             ("wv", wv, wv_s), ("wt", wt, wt_s)):
            for dc in range(8):
                wload.append((nm, dram, sb, dc))

        def emit_wload(n):
            while wload and n > 0:
                nm, dram, sb, dc = wload.pop(0)
                E.op("g", lambda g, dram=dram, sb=sb, dc=dc: g.dma_start(
                    out=sb[:, dc * 1024:(dc + 1) * 1024],
                    in_=dram[dc * 128:(dc + 1) * 128, :]),
                    writes=[nm], dma=True)
                n -= 1

        def stats_part(gk):
            sl = gk % 2
            xbk = f"xbg{sl}"
            sAk, sBk, sCk = f"sA{sl}", f"sB{sl}", f"sC{sl}"
            sA, sB, sC = sAv(sl), sBv(sl), sCv(sl)
            for hh in range(2):
                E.op("g", lambda g, gk=gk, sl=sl, hh=hh: g.dma_start(
                    out=ab[:, sl * 4096 + hh * 2048:
                           sl * 4096 + (hh + 1) * 2048],
                    in_=xTb[:, gk * 4096 + hh * 2048:
                            gk * 4096 + (hh + 1) * 2048]),
                    writes=[xbk + "ab"[hh]], dma=True)
            for hh in range(2):
                E.op("s", lambda s, sl=sl, hh=hh: s.activation(
                    out=ab[:, 8192 + hh * 2048: 8192 + (hh + 1) * 2048],
                    in_=ab[:, sl * 4096 + hh * 2048:
                           sl * 4096 + (hh + 1) * 2048],
                    func=AF.Square),
                    reads=[xbk + "ab"[hh]], writes=["xsq" + "ab"[hh]])
            for hh in range(2):
                E.group("t", [
                    (lambda t, dc=dc, sl=sl: t.matmul(out=pbv[0],
                                                      lhsT=onesd[:],
                                                      rhs=xbg(sl, dc),
                                                      start=dc == 0,
                                                      stop=dc == 7))
                    for dc in range(4 * hh, 4 * hh + 4)],
                    reads=["onesd", xbk + "ab"[hh]], writes=["PB0"])
            for hh in range(2):
                E.group("t", [
                    (lambda t, dc=dc: t.matmul(out=pbv[1], lhsT=onesd[:],
                                               rhs=xsq(dc), start=dc == 0,
                                               stop=dc == 7))
                    for dc in range(4 * hh, 4 * hh + 4)],
                    reads=["onesd", "xsq" + "ab"[hh]], writes=["PB1"])

        def stats_tail(gk):
            sl = gk % 2
            sAk, sBk, sCk = f"sA{sl}", f"sB{sl}", f"sC{sl}"
            sA, sB, sC = sAv(sl), sBv(sl), sCv(sl)
            E.op("s", lambda s, sA=sA: s.copy(out=sA, in_=pbv[0]),
                 reads=["PB0"], writes=[sAk])
            E.op("v", lambda v, sA=sA, sB=sB: v.tensor_mul(out=sB, in0=sA,
                                                           in1=sA),
                 reads=[sAk], writes=[sBk])
            E.op("v", lambda v, sB=sB, sC=sC: v.tensor_sub(out=sC,
                                                           in0=pbv[1],
                                                           in1=sB),
                 reads=["PB1", sBk], writes=[sCk])
            E.op("s", lambda s, sB=sB, sC=sC: s.activation(out=sB, in_=sC,
                                                           func=AF.Ln,
                                                           bias=epsv[:]),
                 reads=[sCk, "epsv"], writes=[sBk])
            E.op("s", lambda s, sB=sB, sC=sC: s.activation(out=sC, in_=sB,
                                                           func=AF.Exp,
                                                           scale=-0.5),
                 reads=[sBk], writes=[sCk])

        def apply_part(gk):
            own = gk >= og0
            sl = gk % 2
            xbk = f"xbg{sl}"
            sAk, sCk = f"sA{sl}", f"sC{sl}"
            sA, sC = sAv(sl), sCv(sl)

            def dest_of(dc):
                if own:
                    return (xno(dc)[:, (gk - og0) * 512:(gk - og0 + 1) * 512],
                            f"xn{dc}")
                return xnt(dc), f"xt{dc}"

            for dc0 in range(0, 8, 2):
                for dc in (dc0, dc0 + 1):
                    sd, sdk = (sD0, "sD0") if dc % 2 == 0 else (sD1, "sD1")
                    E.op("v", lambda v, dc=dc, sl=sl, sA=sA, sd=sd:
                         v.tensor_sub(out=sd, in0=xbg(sl, dc), in1=sA),
                         reads=[xbk + ("a" if dc < 4 else "b"), sAk],
                         writes=[sdk])
                for dc in (dc0, dc0 + 1):
                    sd, sdk = (sD0, "sD0") if dc % 2 == 0 else (sD1, "sD1")
                    dest, dkey = dest_of(dc)
                    E.op("v", lambda v, dest=dest, sC=sC, sd=sd: v.tensor_mul(
                        out=dest, in0=sd, in1=sC),
                        reads=[sdk, sCk], writes=[dkey])

        def k_part(gk):
            own = gk >= og0
            src = (lambda dc, gk=gk: xno(dc)[:, (gk - og0) * 512:(gk - og0 + 1) * 512]) if own \
                else (lambda dc: xnt(dc))

            def skey(dc):
                return f"xn{dc}" if own else f"xt{dc}"
            for jc in range(8):
                bank = 2 + jc % 2
                for dc in range(8):
                    wr = [f"PB{bank}"] if dc in (0, 7) else []
                    E.op("t", lambda t, dc=dc, jc=jc, bank=bank: t.matmul(
                        out=pbv[bank],
                        lhsT=wk_s[:, dc * 1024 + jc * 128: dc * 1024 + (jc + 1) * 128],
                        rhs=src(dc), start=dc == 0, stop=dc == 7),
                        reads=["wk", skey(dc)], writes=wr)
                E.op("s", lambda s, jc=jc, bank=bank, gk=gk: s.activation(
                    out=kT(jc)[:, gk * 512:(gk + 1) * 512], in_=pbv[bank],
                    func=AF.Identity, bias=bias_s[:, 8 + jc: 9 + jc]),
                    reads=[f"PB{bank}", "biasp"], writes=[f"kT{jc}"])

        stats_part(0)
        stats_tail(0)
        for gk in range(kgn):
            if gk + 1 < kgn:
                stats_part(gk + 1)
            apply_part(gk)
            if gk + 1 < kgn:
                stats_tail(gk + 1)
            k_part(gk)
            emit_wload(6 if gk == 0 else 4)
        emit_wload(99)

        arena_src = (["xbg0a", "xbg0b", "xbg1a", "xbg1b", "xsqa", "xsqb",
                      "sD0", "sD1", "wst0", "wst1", "wst2", "wst3"]
                     + [f"sA{p}" for p in range(2)]
                     + [f"sB{p}" for p in range(2)]
                     + [f"sC{p}" for p in range(2)]
                     + [f"xt{dc}" for dc in range(8)])
        E.seal(["q0", "q1", "t1b", "g0", "g1", "xres0", "xres1",
                "out0", "out1", "c10", "c11", "rrow", "vs0", "vs1", "Rsb",
                "expo"],
               src_keys=arena_src)

        xnk = [f"xn{dc}" for dc in range(8)]

        def q_jit(ol, oslot):
            ocol = ol * 512
            qkey = f"q{oslot}"
            for jc in range(8):
                bank = 4 + jc % 2
                E.group("t", [
                    (lambda t, dc=dc, jc=jc, bank=bank, ocol=ocol: t.matmul(
                        out=pb[bank][:],
                        lhsT=wq_s[:, dc * 1024 + jc * 128: dc * 1024 + (jc + 1) * 128],
                        rhs=xno(dc)[:, ocol:ocol + 512],
                        start=dc == 0, stop=dc == 7))
                    for dc in range(8)],
                    reads=["wq"] + xnk, writes=[f"PB{bank}"])
                E.op("s", lambda s, jc=jc, bank=bank, oslot=oslot: s.activation(
                    out=qgrp(oslot, jc), in_=pb[bank][:],
                    func=AF.Identity, bias=bias_s[:, jc:jc + 1]),
                    reads=[f"PB{bank}", "biasp"], writes=[qkey])

        ccnt = [0]
        n_og = (hi - lo) // 4
        seq = list(reversed(range(n_og)))
        if len(seq) > 2:
            seq = [seq[0], seq[-1]] + seq[1:-1]
        q_jit(seq[0], 0)
        for si, ol in enumerate(seq):
            og = og0 + ol
            oslot = si % 2
            ocol = ol * 512
            qkey = f"q{oslot}"

            for i in range(4):
                t_ = og * 4 + i
                kext = 128 * (t_ + 1)
                nch = (kext + 1023) // 1024
                for c in range(nch):
                    w = 1024 if c < nch - 1 else kext - 1024 * (nch - 1)
                    par = ccnt[0] % 2
                    ccnt[0] += 1
                    big = pbBIG[par]
                    bkeys = [f"PB{2 * par}"] + \
                        ([f"PB{2 * par + 1}"] if w > 512 else [])
                    fns = []
                    for jc in range(8):
                        for hh in range(0, w, 512):
                            wh = min(512, w - hh)
                            fns.append(lambda t, jc=jc, c=c, i=i, hh=hh,
                                       wh=wh, big=big: t.matmul(
                                out=big[:, hh:hh + wh],
                                lhsT=qgrp(oslot, jc)[:, i * 128:(i + 1) * 128],
                                rhs=kT(jc)[:, c * 1024 + hh:
                                           c * 1024 + hh + wh],
                                start=jc == 0, stop=jc == 7))
                    E.group("t", fns,
                            reads=[qkey] + [f"kT{jc}" for jc in range(8)],
                            writes=bkeys)
                    if c == nch - 1:
                        m0 = w - 128
                        E.op("v", lambda v, big=big, m0=m0, w=w: v.tensor_mul(
                            out=big[:, m0:w], in0=big[:, m0:w],
                            in1=mask_s[:, 0:w - m0]),
                            reads=bkeys + ["maskp"], writes=bkeys)
                    ns, np_ = c % 2, (c + 1) % 2
                    E.op("s", lambda s, big=big, par=par, w=w: s.activation(
                        out=expo[:, 0:w], in_=big[:, 0:w], func=AF.Exp,
                        scale=INV_DK,
                        accum_out=csum[:, par:par + 1]),
                        reads=bkeys,
                        writes=["expo", f"cs{par}"])
                    if c == 0:
                        E.op("v", lambda v, par=par: v.tensor_copy(
                            out=sumb[:, 0:1], in_=csum[:, par:par + 1]),
                            reads=[f"cs{par}"], writes=["sum0"])
                    else:
                        E.op("v", lambda v, ns=ns, np_=np_, par=par:
                             v.tensor_add(
                                 out=sumb[:, ns:ns + 1],
                                 in0=sumb[:, np_:np_ + 1],
                                 in1=csum[:, par:par + 1]),
                             reads=[f"sum{np_}", f"cs{par}"],
                             writes=[f"sum{ns}"])
                last = (nch - 1) % 2
                zc = float(SEQ - kext)
                E.op("v", lambda v, zc=zc, last=last: v.tensor_scalar_add(
                    out=numb[:], in0=sumb[:, last:last + 1], scalar1=zc),
                    reads=[f"sum{last}"], writes=["num"])
                E.op("v", lambda v: v.reciprocal(out=zib[:], in_=numb[:]),
                     reads=["num"], writes=["zi"])
                E.op("v", lambda v, i=i: v.tensor_mul(
                    out=rgrp[:, i:i + 1], in0=numb[:], in1=zib[:]),
                    reads=["num", "zi"], writes=["rgrp"])
                if i == 2 and si + 1 < n_og:
                    q_jit(seq[si + 1], (si + 1) % 2)


            E.group("t", [
                (lambda t, dc=dc: t.matmul(
                    out=pb[5][0:32, :], lhsT=vr_s[:, dc * 32:(dc + 1) * 32],
                    rhs=xno(dc)[:, ocol:ocol + 512],
                    start=dc == 0, stop=dc == 7))
                for dc in range(8)], reads=["vr"] + xnk, writes=["PB5"])
            E.op("s", lambda s: s.copy(out=t1b[:], in_=pb[5][0:32, :]),
                 reads=["PB5"], writes=["t1b"])
            for oc in range(8):
                p = oc % 2
                wb = 6 if oc % 2 == 0 else 4
                vb = 7 if oc % 2 == 0 else 5
                E.op("y", lambda y, oc=oc, og=og: y.dma_start(
                    out=xres(oc % 2), in_=xTg[:, og * 4096 + oc * 512:
                                              og * 4096 + (oc + 1) * 512]),
                    writes=[f"xres{p}"], dma=True)
                fns = [
                    (lambda t, dc=dc, oc=oc, wb=wb: t.matmul(
                        out=pb[wb][:],
                        lhsT=wt_s[:, dc * 1024 + oc * 128: dc * 1024 + (oc + 1) * 128],
                        rhs=xno(dc)[:, ocol:ocol + 512],
                        start=dc == 0, stop=False))
                    for dc in range(8)]
                fns.append(lambda t, oc=oc, wb=wb: t.matmul(
                    out=pb[wb][:], lhsT=ct_s[:, oc * 128:(oc + 1) * 128],
                    rhs=t1b[:], start=False, stop=True))
                E.group("t", fns, reads=["wt", "ct", "t1b"] + xnk,
                        writes=[f"PB{wb}"])
                E.op("s", lambda s, oc=oc, wb=wb: s.activation(
                    out=grotb(oc % 2), in_=pb[wb][:], func=AF.Silu,
                    bias=bias_s[:, 24 + oc:25 + oc]),
                    reads=[f"PB{wb}", "biasp"], writes=[f"g{p}"])
                E.group("t", [
                    (lambda t, dc=dc, oc=oc, vb=vb: t.matmul(
                        out=pb[vb][:],
                        lhsT=wv_s[:, dc * 1024 + oc * 128: dc * 1024 + (oc + 1) * 128],
                        rhs=xno(dc)[:, ocol:ocol + 512],
                        start=dc == 0, stop=dc == 7))
                    for dc in range(8)], reads=["wv"] + xnk,
                    writes=[f"PB{vb}"])
                if oc == 0:
                    E.group("t", [
                        (lambda t, i=i: t.matmul(
                            out=pb[5][0:1, i * 128:(i + 1) * 128],
                            lhsT=rgrp[:, i:i + 1], rhs=ident_s[:],
                            start=True, stop=True))
                        for i in range(4)],
                        reads=["rgrp", "ident"], writes=["PB5"])
                    E.op("s", lambda s: s.copy(out=rrow[:], in_=pb[5][0:1, :]),
                         reads=["PB5"], writes=["rrow"])
                    E.op("t", lambda t: t.matmul(
                        out=pb[4][:], lhsT=ones1f[:], rhs=rrow[:],
                        start=True, stop=True),
                        reads=["ones1f", "rrow"], writes=["PB4"])
                    E.op("s", lambda s: s.copy(out=Rsb, in_=pb[4][:]),
                         reads=["PB4"], writes=["Rsb"])
                E.op("s", lambda s, oc=oc, vb=vb: s.activation(
                    out=vsb(oc % 2), in_=pb[vb][:], func=AF.Identity,
                    bias=bias_s[:, 16 + oc:17 + oc]),
                    reads=[f"PB{vb}", "biasp"], writes=[f"vs{p}"])
                ee = "g" if oc % 2 == 0 else "v"
                E.op(ee, lambda e, oc=oc: e.tensor_mul(
                    out=c1b(oc % 2), in0=vsb(oc % 2), in1=grotb(oc % 2)),
                    reads=[f"vs{p}", f"g{p}"], writes=[f"c1{p}"])
                E.op(ee, lambda e, oc=oc: e.tensor_mul(
                    out=c1b(oc % 2), in0=c1b(oc % 2), in1=Rsb),
                    reads=[f"c1{p}", "Rsb"], writes=[f"c1{p}"])
                E.op(ee, lambda e, oc=oc: e.tensor_add(
                    out=orot(oc % 2), in0=c1b(oc % 2), in1=xres(oc % 2)),
                    reads=[f"c1{p}", f"xres{p}"], writes=[f"out{p}"])
                E.op("g", lambda g, oc=oc, og=og: g.dma_start(
                    out=outT[oc * 128:(oc + 1) * 128,
                             og * 512:(og + 1) * 512],
                    in_=orot(oc % 2)),
                    reads=[f"out{p}"], dma=True)

        for i in range(Emitter.DMA_DEPTH):
            if E.cnt[f"dx{i}"]:
                nc.sync.wait_ge(sems[f"dx{i}"], E.cnt[f"dx{i}"])
            if E.cnt[f"dw{i}"]:
                nc.gpsimd.wait_ge(sems[f"dw{i}"], E.cnt[f"dw{i}"])

    if single_body is not None:
        emit_body(*((0, A_TILES) if single_body == 0 else (A_TILES, NT)))
    else:
        engines = [nc.tensor, nc.vector, nc.scalar, nc.sync, nc.gpsimd]
        idx = [e.partition_id() % 2 for e in engines]
        for b in nc.Switch(engines=engines, index=idx, n=2):
            if b == 0:
                emit_body(0, A_TILES)
            else:
                emit_body(A_TILES, NT)
    return nc


def _prep(X, Wq, bq, Wk, bk, Wv, bv, W, b, V_r, C):
    BF = ml_dtypes.bfloat16
    xTg = [np.ascontiguousarray(
        X[bb].reshape(8, 512, 8, 128).transpose(3, 0, 2, 1).reshape(128, 8 * SEQ))
        for bb in range(4)]
    xTb = [t.astype(BF) for t in xTg]
    biasp = np.empty((128, 32), dtype=np.float32)
    for wi, bb in enumerate((bq, bk, bv, b)):
        biasp[:, wi * 8:(wi + 1) * 8] = np.asarray(bb, np.float32).reshape(8, 128).T
    maskp = (np.arange(512)[None, :] <= np.arange(128)[:, None]).astype(BF)
    ident = np.eye(128, dtype=np.float32)
    shared = {
        "wq": np.ascontiguousarray(np.asarray(Wq, np.float32).astype(BF)),
        "wk": np.ascontiguousarray(np.asarray(Wk, np.float32).astype(BF)),
        "wv": np.ascontiguousarray(np.asarray(Wv, np.float32).astype(BF)),
        "wt": np.ascontiguousarray(np.asarray(W, np.float32).T.astype(BF)),
        "vr": np.ascontiguousarray(np.asarray(V_r, np.float32).astype(BF)),
        "ct": np.ascontiguousarray(np.asarray(C, np.float32).T.astype(BF)),
        "biasp": biasp, "maskp": maskp, "ident": ident,
    }
    return xTg, xTb, shared


def kernel(L0_attn_X, L0_attn_Wq, L0_attn_bq, L0_attn_Wk, L0_attn_bk,
           L0_attn_Wv, L0_attn_bv, L0_ffn_W, L0_ffn_b, L0_ffn_V_r, L0_ffn_C):
    X = np.asarray(L0_attn_X, np.float32)
    xTg, xTb, shared = _prep(X, L0_attn_Wq, L0_attn_bq, L0_attn_Wk,
                             L0_attn_bk, L0_attn_Wv, L0_attn_bv, L0_ffn_W,
                             L0_ffn_b, L0_ffn_V_r, L0_ffn_C)
    if "nc" not in _CACHE:
        _CACHE["nc"] = build_program()
    nc = _CACHE["nc"]
    in_maps = [dict(shared, xTg=xTg[c // 2], xTb=xTb[c // 2])
               for c in range(8)]
    res = run_bass_kernel_spmd(nc, in_maps, core_ids=list(range(8)))
    split = 128 * A_TILES
    out = np.empty((4, SEQ, D), dtype=np.float32)
    for bb in range(4):
        oa = np.asarray(res.results[2 * bb]["outT"])
        ob = np.asarray(res.results[2 * bb + 1]["outT"])
        out[bb, :split, :] = oa[:, :split].T
        out[bb, split:, :] = ob[:, split:].T
    return out


# version 28
# speedup vs baseline: 1.2561x; 1.0033x over previous
import math
import numpy as np
import ml_dtypes
import concourse.bass as bass
import concourse.mybir as mybir
from concourse.bass_utils import run_bass_kernel_spmd

F32 = mybir.dt.float32
BF16 = mybir.dt.bfloat16
AF = mybir.ActivationFunctionType
ALU = mybir.AluOpType
AX = mybir.AxisListType

A_TILES = 20
NT = 32
D = 1024
SEQ = 4096
INV_DK = 1.0 / 32.0


class Emitter:

    DMA_DEPTH = 6

    def __init__(self, nc, sems):
        self.nc = nc
        self.sems = sems
        self.cnt = {k: 0 for k in sems}
        self.lastw = {}
        self.readers = {}
        self.waited = {}
        self.dmasub = {"dx": 0, "dw": 0}
        self.eng = {
            "t": nc.tensor, "v": nc.vector, "s": nc.scalar,
            "y": nc.sync, "g": nc.gpsimd,
        }

    def _semname(self, e, dma):
        if e == "y":
            return "dx"
        if e == "g":
            return "dw" if dma else "gc"
        return {"t": "tt", "v": "vv", "s": "ss"}[e]

    def _wait(self, e, tok):
        sn, val = tok
        if e == "t" and sn == "tt":
            return
        w = self.waited.setdefault(e, {})
        if w.get(sn, 0) >= val:
            return
        self.eng[e].wait_ge(self.sems[sn], val)
        w[sn] = val

    def _deps(self, e, reads, writes):
        toks = []
        for r in reads:
            t = self.lastw.get(r)
            if t:
                toks.append(t)
        for wkey in writes:
            t = self.lastw.get(wkey)
            if t:
                toks.append(t)
            toks.extend(self.readers.get(wkey, ()))
        best = {}
        for sn, val in toks:
            if best.get(sn, 0) < val:
                best[sn] = val
        for sn, val in best.items():
            self._wait(e, (sn, val))

    def _finish(self, e, inst, reads, writes, dma):
        sn = self._semname(e, dma)
        inc = 16 if dma else 1
        inst.then_inc(self.sems[sn], inc)
        self.cnt[sn] += inc
        tok = (sn, self.cnt[sn])
        for wkey in writes:
            self.lastw[wkey] = tok
            self.readers[wkey] = []
        for r in reads:
            self.readers.setdefault(r, []).append(tok)
        return tok

    def op(self, e, fn, reads=(), writes=(), dma=False):
        self._deps(e, reads, writes)
        if dma:
            q = self._semname(e, True)
            sn = f"{q}{self.dmasub[q]}"
            self.dmasub[q] = (self.dmasub[q] + 1) % self.DMA_DEPTH
            if self.cnt[sn] > 0:
                self._wait(e, (sn, self.cnt[sn]))
            inst = fn(self.eng[e])
            inst.then_inc(self.sems[sn], 16)
            self.cnt[sn] += 16
            tok = (sn, self.cnt[sn])
            for wkey in writes:
                self.lastw[wkey] = tok
                self.readers[wkey] = []
            for r in reads:
                self.readers.setdefault(r, []).append(tok)
            return tok
        inst = fn(self.eng[e])
        return self._finish(e, inst, reads, writes, dma)

    def group(self, e, fns, reads=(), writes=()):
        self._deps(e, reads, writes)
        inst = None
        for fn in fns:
            inst = fn(self.eng[e])
        return self._finish(e, inst, reads, writes, False)

    def seal(self, keys, src_keys=None):
        if src_keys is None:
            toks = [(sn, v) for sn, v in self.cnt.items() if v > 0]
        else:
            best = {}
            for k2 in src_keys:
                cand = list(self.readers.get(k2, ()))
                t = self.lastw.get(k2)
                if t:
                    cand.append(t)
                for sn, v in cand:
                    if best.get(sn, 0) < v:
                        best[sn] = v
            toks = list(best.items())
        for k in keys:
            self.readers[k] = list(toks)
            self.lastw.pop(k, None)


_CACHE = {}


def build_program(single_body=None):
    nc = bass.Bass("TRN2", target_bir_lowering=False)

    xTg = nc.dram_tensor("xTg", [128, 8 * SEQ], F32, kind="ExternalInput")
    xTb = nc.dram_tensor("xTb", [128, 8 * SEQ], BF16, kind="ExternalInput")
    wq = nc.dram_tensor("wq", [D, D], BF16, kind="ExternalInput")
    wk = nc.dram_tensor("wk", [D, D], BF16, kind="ExternalInput")
    wv = nc.dram_tensor("wv", [D, D], BF16, kind="ExternalInput")
    wt = nc.dram_tensor("wt", [D, D], BF16, kind="ExternalInput")
    vrd = nc.dram_tensor("vr", [D, 32], BF16, kind="ExternalInput")
    ctd = nc.dram_tensor("ct", [32, D], BF16, kind="ExternalInput")
    biasd = nc.dram_tensor("biasp", [128, 32], F32, kind="ExternalInput")
    maskd = nc.dram_tensor("maskp", [128, 512], BF16, kind="ExternalInput")
    identd = nc.dram_tensor("ident", [128, 128], F32, kind="ExternalInput")
    outT = nc.dram_tensor("outT", [D, SEQ], F32, kind="ExternalOutput")

    wq_s = nc.alloc_sbuf_tensor("wq_s", [128, 8192], BF16)
    wk_s = nc.alloc_sbuf_tensor("wk_s", [128, 8192], BF16)
    wv_s = nc.alloc_sbuf_tensor("wv_s", [128, 8192], BF16)
    wt_s = nc.alloc_sbuf_tensor("wt_s", [128, 8192], BF16)
    vr_s = nc.alloc_sbuf_tensor("vr_s", [128, 256], BF16)
    ct_s = nc.alloc_sbuf_tensor("ct_s", [32, 1024], BF16)
    bias_s = nc.alloc_sbuf_tensor("bias_s", [128, 32], F32)
    mask_s = nc.alloc_sbuf_tensor("mask_s", [128, 512], BF16)
    ident_s = nc.alloc_sbuf_tensor("ident_s", [128, 128], F32)
    onesd = nc.alloc_sbuf_tensor("onesd", [128, 128], BF16)
    ones1f = nc.alloc_sbuf_tensor("ones1f", [1, 128], F32)
    epsv = nc.alloc_sbuf_tensor("epsv", [128, 1], F32)
    rbig = nc.alloc_sbuf_tensor("rbig", [128, 45056], BF16)
    ab = nc.alloc_sbuf_tensor("ab", [128, 16384], BF16)
    af = nc.alloc_sbuf_tensor("af", [128, 4608], F32)
    cmax = nc.alloc_sbuf_tensor("cmax", [128, 1], F32)
    nmb = nc.alloc_sbuf_tensor("nmb", [128, 2], F32)
    sumb = nc.alloc_sbuf_tensor("sumb", [128, 2], F32)
    csum = nc.alloc_sbuf_tensor("csum", [128, 2], F32)
    dbuf = nc.alloc_sbuf_tensor("dbuf", [128, 1], F32)
    fbuf = nc.alloc_sbuf_tensor("fbuf", [128, 1], F32)
    ezb = nc.alloc_sbuf_tensor("ezb", [128, 1], F32)
    numb = nc.alloc_sbuf_tensor("numb", [128, 1], F32)
    zib = nc.alloc_sbuf_tensor("zib", [128, 1], F32)
    rgrp = nc.alloc_sbuf_tensor("rgrp", [128, 4], F32)

    pb01 = nc.alloc_psum_tensor("pb01", [128, 1024], F32)
    pb23 = nc.alloc_psum_tensor("pb23", [128, 1024], F32)
    pbv = {0: pb01[:, 0:512], 1: pb01[:, 512:1024],
           2: pb23[:, 0:512], 3: pb23[:, 512:1024]}
    pb = {i: nc.alloc_psum_tensor(f"pb{i}", [128, 512], F32)
          for i in range(4, 8)}
    pbBIG = [pb01, pb23]

    sems = {
        "tt": nc.alloc_semaphore("tt"), "vv": nc.alloc_semaphore("vv"),
        "ss": nc.alloc_semaphore("ss"), "gc": nc.alloc_semaphore("gc"),
    }
    for q in ("dx", "dw"):
        for i in range(Emitter.DMA_DEPTH):
            sems[f"{q}{i}"] = nc.alloc_semaphore(f"{q}{i}")

    def emit_body(lo, hi):
        E = Emitter(nc, sems)
        kgn = (hi + 3) // 4
        og0 = lo // 4
        kcols = kgn * 512
        ownlen = 128 * (hi - lo)

        def kT(jc):
            return rbig[:, jc * kcols:(jc + 1) * kcols]

        def xno(dc):
            base = 8 * kcols
            return rbig[:, base + dc * ownlen: base + (dc + 1) * ownlen]

        def xbg(slot, dc):
            return ab[:, slot * 4096 + dc * 512: slot * 4096 + (dc + 1) * 512]

        def xbg_all(slot):
            return ab[:, slot * 4096:(slot + 1) * 4096]

        def xsq(dc):
            return ab[:, 8192 + dc * 512: 8192 + (dc + 1) * 512]
        xsq_all = ab[:, 8192:12288]

        def xnt(dc):
            return ab[:, 12288 + dc * 512: 12288 + (dc + 1) * 512]

        def qgrp(slot, jc):
            return ab[:, slot * 4096 + jc * 512: slot * 4096 + (jc + 1) * 512]
        t1b = ab[0:32, 8192:8704]

        def grotb(p):
            return ab[:, 8704 + p * 512: 8704 + (p + 1) * 512]
        Rsb = ab[:, 10240:10752]
        expo = ab[:, 10752:11776]

        def sAv(sl):
            return af[:, sl * 1536: sl * 1536 + 512]

        def sBv(sl):
            return af[:, sl * 1536 + 512: sl * 1536 + 1024]

        def sCv(sl):
            return af[:, sl * 1536 + 1024: sl * 1536 + 1536]
        sD0 = af[:, 3072:3584]
        sD1 = af[:, 3584:4096]

        def wstage(p):
            return af[:, 2048 + p * 512: 2048 + (p + 1) * 512]

        def vsb(p):
            return af[:, p * 512:(p + 1) * 512]

        def xres(p):
            return af[:, 1024 + p * 512: 1024 + (p + 1) * 512]

        def orot(p):
            return af[:, 2048 + p * 512: 2048 + (p + 1) * 512]

        def c1b(p):
            return af[:, 3072 + p * 512: 3072 + (p + 1) * 512]
        rrow = af[0:1, 4096:4608]

        E.op("g", lambda g: g.dma_start(out=epsv[0:1, 0:1], in_=xTb[0:1, 0:1]),
             writes=["epsv"], dma=True)
        E.op("v", lambda v: v.memset(onesd[:], 1.0 / 1024.0), writes=["onesd"])
        E.op("v", lambda v: v.memset(ones1f[:], 1.0), writes=["ones1f"])
        E.op("v", lambda v: v.memset(epsv[:], 1e-5), writes=["epsv"])
        for dc in range(8):
            E.op("y", lambda y, dc=dc: y.dma_start(
                out=wk_s[:, dc * 1024:(dc + 1) * 1024],
                in_=wk[dc * 128:(dc + 1) * 128, :]),
                writes=["wk"], dma=True)
        E.op("y", lambda y: y.dma_start(out=bias_s[:], in_=biasd[:]),
             writes=["biasp"], dma=True)
        E.op("y", lambda y: y.dma_start(out=ident_s[:], in_=identd[:]),
             writes=["ident"], dma=True)
        E.op("y", lambda y: y.dma_start(out=mask_s[:], in_=maskd[:]),
             writes=["maskp"], dma=True)
        for dc in range(8):
            E.op("y", lambda y, dc=dc: y.dma_start(
                out=vr_s[:, dc * 32:(dc + 1) * 32],
                in_=vrd[dc * 128:(dc + 1) * 128, :]),
                writes=["vr"], dma=True)
        E.op("y", lambda y: y.dma_start(out=ct_s[:], in_=ctd[:]),
             writes=["ct"], dma=True)

        wload = []
        for nm, dram, sb in (("wq", wq, wq_s),
                             ("wv", wv, wv_s), ("wt", wt, wt_s)):
            for dc in range(8):
                wload.append((nm, dram, sb, dc))

        def emit_wload(n):
            while wload and n > 0:
                nm, dram, sb, dc = wload.pop(0)
                E.op("g", lambda g, dram=dram, sb=sb, dc=dc: g.dma_start(
                    out=sb[:, dc * 1024:(dc + 1) * 1024],
                    in_=dram[dc * 128:(dc + 1) * 128, :]),
                    writes=[nm], dma=True)
                n -= 1

        def stats_part(gk):
            sl = gk % 2
            xbk = f"xbg{sl}"
            sAk, sBk, sCk = f"sA{sl}", f"sB{sl}", f"sC{sl}"
            sA, sB, sC = sAv(sl), sBv(sl), sCv(sl)
            for hh in range(2):
                E.op("g", lambda g, gk=gk, sl=sl, hh=hh: g.dma_start(
                    out=ab[:, sl * 4096 + hh * 2048:
                           sl * 4096 + (hh + 1) * 2048],
                    in_=xTb[:, gk * 4096 + hh * 2048:
                            gk * 4096 + (hh + 1) * 2048]),
                    writes=[xbk + "ab"[hh]], dma=True)
            for hh in range(2):
                E.op("s", lambda s, sl=sl, hh=hh: s.activation(
                    out=ab[:, 8192 + hh * 2048: 8192 + (hh + 1) * 2048],
                    in_=ab[:, sl * 4096 + hh * 2048:
                           sl * 4096 + (hh + 1) * 2048],
                    func=AF.Square),
                    reads=[xbk + "ab"[hh]], writes=["xsq" + "ab"[hh]])
            for hh in range(2):
                E.group("t", [
                    (lambda t, dc=dc, sl=sl: t.matmul(out=pbv[0],
                                                      lhsT=onesd[:],
                                                      rhs=xbg(sl, dc),
                                                      start=dc == 0,
                                                      stop=dc == 7))
                    for dc in range(4 * hh, 4 * hh + 4)],
                    reads=["onesd", xbk + "ab"[hh]], writes=["PB0"])
            for hh in range(2):
                E.group("t", [
                    (lambda t, dc=dc: t.matmul(out=pbv[1], lhsT=onesd[:],
                                               rhs=xsq(dc), start=dc == 0,
                                               stop=dc == 7))
                    for dc in range(4 * hh, 4 * hh + 4)],
                    reads=["onesd", "xsq" + "ab"[hh]], writes=["PB1"])

        def stats_tail(gk):
            sl = gk % 2
            sAk, sBk, sCk = f"sA{sl}", f"sB{sl}", f"sC{sl}"
            sA, sB, sC = sAv(sl), sBv(sl), sCv(sl)
            E.op("s", lambda s, sA=sA: s.copy(out=sA, in_=pbv[0]),
                 reads=["PB0"], writes=[sAk])
            E.op("v", lambda v, sA=sA, sB=sB: v.tensor_mul(out=sB, in0=sA,
                                                           in1=sA),
                 reads=[sAk], writes=[sBk])
            E.op("v", lambda v, sB=sB, sC=sC: v.tensor_sub(out=sC,
                                                           in0=pbv[1],
                                                           in1=sB),
                 reads=["PB1", sBk], writes=[sCk])
            E.op("s", lambda s, sB=sB, sC=sC: s.activation(out=sB, in_=sC,
                                                           func=AF.Ln,
                                                           bias=epsv[:]),
                 reads=[sCk, "epsv"], writes=[sBk])
            E.op("s", lambda s, sB=sB, sC=sC: s.activation(out=sC, in_=sB,
                                                           func=AF.Exp,
                                                           scale=-0.5),
                 reads=[sBk], writes=[sCk])

        def apply_part(gk):
            own = gk >= og0
            sl = gk % 2
            xbk = f"xbg{sl}"
            sAk, sCk = f"sA{sl}", f"sC{sl}"
            sA, sC = sAv(sl), sCv(sl)

            def dest_of(dc):
                if own:
                    return (xno(dc)[:, (gk - og0) * 512:(gk - og0 + 1) * 512],
                            f"xn{dc}")
                return xnt(dc), f"xt{dc}"

            for dc0 in range(0, 8, 2):
                for dc in (dc0, dc0 + 1):
                    sd, sdk = (sD0, "sD0") if dc % 2 == 0 else (sD1, "sD1")
                    E.op("v", lambda v, dc=dc, sl=sl, sA=sA, sd=sd:
                         v.tensor_sub(out=sd, in0=xbg(sl, dc), in1=sA),
                         reads=[xbk + ("a" if dc < 4 else "b"), sAk],
                         writes=[sdk])
                for dc in (dc0, dc0 + 1):
                    sd, sdk = (sD0, "sD0") if dc % 2 == 0 else (sD1, "sD1")
                    dest, dkey = dest_of(dc)
                    E.op("v", lambda v, dest=dest, sC=sC, sd=sd: v.tensor_mul(
                        out=dest, in0=sd, in1=sC),
                        reads=[sdk, sCk], writes=[dkey])

        def k_part(gk):
            own = gk >= og0
            src = (lambda dc, gk=gk: xno(dc)[:, (gk - og0) * 512:(gk - og0 + 1) * 512]) if own \
                else (lambda dc: xnt(dc))

            def skey(dc):
                return f"xn{dc}" if own else f"xt{dc}"
            for jc in range(8):
                bank = 2 + jc % 2
                for dc in range(8):
                    wr = [f"PB{bank}"] if dc in (0, 7) else []
                    E.op("t", lambda t, dc=dc, jc=jc, bank=bank: t.matmul(
                        out=pbv[bank],
                        lhsT=wk_s[:, dc * 1024 + jc * 128: dc * 1024 + (jc + 1) * 128],
                        rhs=src(dc), start=dc == 0, stop=dc == 7),
                        reads=["wk", skey(dc)], writes=wr)
                E.op("s", lambda s, jc=jc, bank=bank, gk=gk: s.activation(
                    out=kT(jc)[:, gk * 512:(gk + 1) * 512], in_=pbv[bank],
                    func=AF.Identity, bias=bias_s[:, 8 + jc: 9 + jc]),
                    reads=[f"PB{bank}", "biasp"], writes=[f"kT{jc}"])

        stats_part(0)
        stats_tail(0)
        for gk in range(kgn):
            if gk + 1 < kgn:
                stats_part(gk + 1)
            apply_part(gk)
            if gk + 1 < kgn:
                stats_tail(gk + 1)
            k_part(gk)
            emit_wload(6 if gk == 0 else 4)
        emit_wload(99)

        arena_src = (["xbg0a", "xbg0b", "xbg1a", "xbg1b", "xsqa", "xsqb",
                      "sD0", "sD1", "wst0", "wst1", "wst2", "wst3"]
                     + [f"sA{p}" for p in range(2)]
                     + [f"sB{p}" for p in range(2)]
                     + [f"sC{p}" for p in range(2)]
                     + [f"xt{dc}" for dc in range(8)])
        E.seal(["q0", "q1", "t1b", "g0", "g1", "xres0", "xres1",
                "out0", "out1", "c10", "c11", "rrow", "vs0", "vs1", "Rsb",
                "expo"],
               src_keys=arena_src)

        xnk = [f"xn{dc}" for dc in range(8)]

        def q_jit(ol, oslot):
            ocol = ol * 512
            qkey = f"q{oslot}"
            for jc in range(8):
                bank = 4 + jc % 2
                E.group("t", [
                    (lambda t, dc=dc, jc=jc, bank=bank, ocol=ocol: t.matmul(
                        out=pb[bank][:],
                        lhsT=wq_s[:, dc * 1024 + jc * 128: dc * 1024 + (jc + 1) * 128],
                        rhs=xno(dc)[:, ocol:ocol + 512],
                        start=dc == 0, stop=dc == 7))
                    for dc in range(8)],
                    reads=["wq"] + xnk, writes=[f"PB{bank}"])
                E.op("s", lambda s, jc=jc, bank=bank, oslot=oslot: s.activation(
                    out=qgrp(oslot, jc), in_=pb[bank][:],
                    func=AF.Identity, bias=bias_s[:, jc:jc + 1]),
                    reads=[f"PB{bank}", "biasp"], writes=[qkey])

        ccnt = [0]
        n_og = (hi - lo) // 4
        seq = list(reversed(range(n_og)))
        if len(seq) > 2:
            seq = [seq[0], seq[-1]] + seq[1:-1]
        q_jit(seq[0], 0)
        for si, ol in enumerate(seq):
            og = og0 + ol
            oslot = si % 2
            ocol = ol * 512
            qkey = f"q{oslot}"

            for i in range(4):
                t_ = og * 4 + i
                kext = 128 * (t_ + 1)
                nch = (kext + 1023) // 1024
                for c in range(nch):
                    w = 1024 if c < nch - 1 else kext - 1024 * (nch - 1)
                    par = ccnt[0] % 2
                    ccnt[0] += 1
                    big = pbBIG[par]
                    bkeys = [f"PB{2 * par}"] + \
                        ([f"PB{2 * par + 1}"] if w > 512 else [])
                    fns = []
                    for jc in range(8):
                        for hh in range(0, w, 512):
                            wh = min(512, w - hh)
                            fns.append(lambda t, jc=jc, c=c, i=i, hh=hh,
                                       wh=wh, big=big: t.matmul(
                                out=big[:, hh:hh + wh],
                                lhsT=qgrp(oslot, jc)[:, i * 128:(i + 1) * 128],
                                rhs=kT(jc)[:, c * 1024 + hh:
                                           c * 1024 + hh + wh],
                                start=jc == 0, stop=jc == 7))
                    E.group("t", fns,
                            reads=[qkey] + [f"kT{jc}" for jc in range(8)],
                            writes=bkeys)
                    if c == nch - 1:
                        m0 = w - 128
                        E.op("v", lambda v, big=big, m0=m0, w=w: v.tensor_mul(
                            out=big[:, m0:w], in0=big[:, m0:w],
                            in1=mask_s[:, 0:w - m0]),
                            reads=bkeys + ["maskp"], writes=bkeys)
                    ns, np_ = c % 2, (c + 1) % 2
                    E.op("s", lambda s, big=big, par=par, w=w: s.activation(
                        out=expo[:, 0:w], in_=big[:, 0:w], func=AF.Exp,
                        scale=INV_DK,
                        accum_out=csum[:, par:par + 1]),
                        reads=bkeys,
                        writes=["expo", f"cs{par}"])
                    if c == 0:
                        E.op("v", lambda v, par=par: v.tensor_copy(
                            out=sumb[:, 0:1], in_=csum[:, par:par + 1]),
                            reads=[f"cs{par}"], writes=["sum0"])
                    else:
                        E.op("v", lambda v, ns=ns, np_=np_, par=par:
                             v.tensor_add(
                                 out=sumb[:, ns:ns + 1],
                                 in0=sumb[:, np_:np_ + 1],
                                 in1=csum[:, par:par + 1]),
                             reads=[f"sum{np_}", f"cs{par}"],
                             writes=[f"sum{ns}"])
                last = (nch - 1) % 2
                zc = float(SEQ - kext)
                E.op("v", lambda v, zc=zc, last=last: v.tensor_scalar_add(
                    out=numb[:], in0=sumb[:, last:last + 1], scalar1=zc),
                    reads=[f"sum{last}"], writes=["num"])
                E.op("v", lambda v: v.reciprocal(out=zib[:], in_=numb[:]),
                     reads=["num"], writes=["zi"])
                E.op("v", lambda v, i=i: v.tensor_mul(
                    out=rgrp[:, i:i + 1], in0=numb[:], in1=zib[:]),
                    reads=["num", "zi"], writes=["rgrp"])
                if i == 2 and si + 1 < n_og:
                    q_jit(seq[si + 1], (si + 1) % 2)


            E.group("t", [
                (lambda t, dc=dc: t.matmul(
                    out=pb[5][0:32, :], lhsT=vr_s[:, dc * 32:(dc + 1) * 32],
                    rhs=xno(dc)[:, ocol:ocol + 512],
                    start=dc == 0, stop=dc == 7))
                for dc in range(8)], reads=["vr"] + xnk, writes=["PB5"])
            E.op("s", lambda s: s.copy(out=t1b[:], in_=pb[5][0:32, :]),
                 reads=["PB5"], writes=["t1b"])
            for oc in range(8):
                p = oc % 2
                wb = 6 if oc % 2 == 0 else 4
                vb = 7 if oc % 2 == 0 else 5
                E.op("y", lambda y, oc=oc, og=og: y.dma_start(
                    out=xres(oc % 2), in_=xTg[:, og * 4096 + oc * 512:
                                              og * 4096 + (oc + 1) * 512]),
                    writes=[f"xres{p}"], dma=True)
                fns = [
                    (lambda t, dc=dc, oc=oc, wb=wb: t.matmul(
                        out=pb[wb][:],
                        lhsT=wt_s[:, dc * 1024 + oc * 128: dc * 1024 + (oc + 1) * 128],
                        rhs=xno(dc)[:, ocol:ocol + 512],
                        start=dc == 0, stop=False))
                    for dc in range(8)]
                fns.append(lambda t, oc=oc, wb=wb: t.matmul(
                    out=pb[wb][:], lhsT=ct_s[:, oc * 128:(oc + 1) * 128],
                    rhs=t1b[:], start=False, stop=True))
                E.group("t", fns, reads=["wt", "ct", "t1b"] + xnk,
                        writes=[f"PB{wb}"])
                E.op("s", lambda s, oc=oc, wb=wb: s.activation(
                    out=grotb(oc % 2), in_=pb[wb][:], func=AF.Silu,
                    bias=bias_s[:, 24 + oc:25 + oc]),
                    reads=[f"PB{wb}", "biasp"], writes=[f"g{p}"])
                E.group("t", [
                    (lambda t, dc=dc, oc=oc, vb=vb: t.matmul(
                        out=pb[vb][:],
                        lhsT=wv_s[:, dc * 1024 + oc * 128: dc * 1024 + (oc + 1) * 128],
                        rhs=xno(dc)[:, ocol:ocol + 512],
                        start=dc == 0, stop=dc == 7))
                    for dc in range(8)], reads=["wv"] + xnk,
                    writes=[f"PB{vb}"])
                if oc == 0:
                    E.group("t", [
                        (lambda t, i=i: t.matmul(
                            out=pb[5][0:1, i * 128:(i + 1) * 128],
                            lhsT=rgrp[:, i:i + 1], rhs=ident_s[:],
                            start=True, stop=True))
                        for i in range(4)],
                        reads=["rgrp", "ident"], writes=["PB5"])
                    E.op("s", lambda s: s.copy(out=rrow[:], in_=pb[5][0:1, :]),
                         reads=["PB5"], writes=["rrow"])
                    E.op("t", lambda t: t.matmul(
                        out=pb[4][:], lhsT=ones1f[:], rhs=rrow[:],
                        start=True, stop=True),
                        reads=["ones1f", "rrow"], writes=["PB4"])
                    E.op("s", lambda s: s.copy(out=Rsb, in_=pb[4][:]),
                         reads=["PB4"], writes=["Rsb"])
                E.op("s", lambda s, oc=oc, vb=vb: s.activation(
                    out=vsb(oc % 2), in_=pb[vb][:], func=AF.Identity,
                    bias=bias_s[:, 16 + oc:17 + oc]),
                    reads=[f"PB{vb}", "biasp"], writes=[f"vs{p}"])
                ee = "g" if oc % 2 == 0 else "v"
                E.op(ee, lambda e, oc=oc: e.tensor_mul(
                    out=c1b(oc % 2), in0=vsb(oc % 2), in1=grotb(oc % 2)),
                    reads=[f"vs{p}", f"g{p}"], writes=[f"c1{p}"])
                E.op(ee, lambda e, oc=oc: e.tensor_mul(
                    out=c1b(oc % 2), in0=c1b(oc % 2), in1=Rsb),
                    reads=[f"c1{p}", "Rsb"], writes=[f"c1{p}"])
                E.op(ee, lambda e, oc=oc: e.tensor_add(
                    out=orot(oc % 2), in0=c1b(oc % 2), in1=xres(oc % 2)),
                    reads=[f"c1{p}", f"xres{p}"], writes=[f"out{p}"])
                oq = "y" if (si == n_og - 1 and oc >= 5) else "g"
                E.op(oq, lambda q, oc=oc, og=og: q.dma_start(
                    out=outT[oc * 128:(oc + 1) * 128,
                             og * 512:(og + 1) * 512],
                    in_=orot(oc % 2)),
                    reads=[f"out{p}"], dma=True)

        for i in range(Emitter.DMA_DEPTH):
            if E.cnt[f"dx{i}"]:
                nc.sync.wait_ge(sems[f"dx{i}"], E.cnt[f"dx{i}"])
            if E.cnt[f"dw{i}"]:
                nc.gpsimd.wait_ge(sems[f"dw{i}"], E.cnt[f"dw{i}"])

    if single_body is not None:
        emit_body(*((0, A_TILES) if single_body == 0 else (A_TILES, NT)))
    else:
        engines = [nc.tensor, nc.vector, nc.scalar, nc.sync, nc.gpsimd]
        idx = [e.partition_id() % 2 for e in engines]
        for b in nc.Switch(engines=engines, index=idx, n=2):
            if b == 0:
                emit_body(0, A_TILES)
            else:
                emit_body(A_TILES, NT)
    return nc


def _prep(X, Wq, bq, Wk, bk, Wv, bv, W, b, V_r, C):
    BF = ml_dtypes.bfloat16
    xTg = [np.ascontiguousarray(
        X[bb].reshape(8, 512, 8, 128).transpose(3, 0, 2, 1).reshape(128, 8 * SEQ))
        for bb in range(4)]
    xTb = [t.astype(BF) for t in xTg]
    biasp = np.empty((128, 32), dtype=np.float32)
    for wi, bb in enumerate((bq, bk, bv, b)):
        biasp[:, wi * 8:(wi + 1) * 8] = np.asarray(bb, np.float32).reshape(8, 128).T
    maskp = (np.arange(512)[None, :] <= np.arange(128)[:, None]).astype(BF)
    ident = np.eye(128, dtype=np.float32)
    shared = {
        "wq": np.ascontiguousarray(np.asarray(Wq, np.float32).astype(BF)),
        "wk": np.ascontiguousarray(np.asarray(Wk, np.float32).astype(BF)),
        "wv": np.ascontiguousarray(np.asarray(Wv, np.float32).astype(BF)),
        "wt": np.ascontiguousarray(np.asarray(W, np.float32).T.astype(BF)),
        "vr": np.ascontiguousarray(np.asarray(V_r, np.float32).astype(BF)),
        "ct": np.ascontiguousarray(np.asarray(C, np.float32).T.astype(BF)),
        "biasp": biasp, "maskp": maskp, "ident": ident,
    }
    return xTg, xTb, shared


def kernel(L0_attn_X, L0_attn_Wq, L0_attn_bq, L0_attn_Wk, L0_attn_bk,
           L0_attn_Wv, L0_attn_bv, L0_ffn_W, L0_ffn_b, L0_ffn_V_r, L0_ffn_C):
    X = np.asarray(L0_attn_X, np.float32)
    xTg, xTb, shared = _prep(X, L0_attn_Wq, L0_attn_bq, L0_attn_Wk,
                             L0_attn_bk, L0_attn_Wv, L0_attn_bv, L0_ffn_W,
                             L0_ffn_b, L0_ffn_V_r, L0_ffn_C)
    if "nc" not in _CACHE:
        _CACHE["nc"] = build_program()
    nc = _CACHE["nc"]
    in_maps = [dict(shared, xTg=xTg[c // 2], xTb=xTb[c // 2])
               for c in range(8)]
    res = run_bass_kernel_spmd(nc, in_maps, core_ids=list(range(8)))
    split = 128 * A_TILES
    out = np.empty((4, SEQ, D), dtype=np.float32)
    for bb in range(4):
        oa = np.asarray(res.results[2 * bb]["outT"])
        ob = np.asarray(res.results[2 * bb + 1]["outT"])
        out[bb, :split, :] = oa[:, :split].T
        out[bb, split:, :] = ob[:, split:].T
    return out
